# version 17
# baseline (speedup 1.0000x reference)
import contextlib
import numpy as np
import ml_dtypes

import concourse.bass as bass
import concourse.mybir as mybir
from concourse.bass_utils import run_bass_kernel_spmd

F32 = mybir.dt.float32
BF16 = mybir.dt.bfloat16
I32 = mybir.dt.int32
AF = mybir.ActivationFunctionType
ALU = mybir.AluOpType
BF = ml_dtypes.bfloat16

NCORE = 8
D = 1024
NCH = 8
K = 32
M = 8
REP = 8
KS = 4
B = 4
L = 4096
LS = L // NCORE
HALO = 4
LT = LS + HALO
T = B * LT
PI = float(np.pi)
NROUND = B * REP
TOTC = 17
RING = 2


class Sem:
    def __init__(self, name):
        self.name = name
        self.n = 0
        self.h = None


def build_program():
    nc = bass.Bass("TRN2", target_bir_lowering=False)

    dr = {}
    def di(name, shape):
        dr[name] = nc.dram_tensor(name, list(shape), F32, kind="ExternalInput").ap()

    di("x", (128, NCH, T))
    di("w_in", (24, NCH, 128, 128))
    di("w_in_s", (D, 32))
    di("w_out", (REP, NCH, 128, 128))
    for nm in ("wf", "wi", "wc", "wo", "uf", "ui", "uc", "uo"):
        di(nm, (NCH, NCH, 128, 128))
    di("bias3", (128, NCH, 3))
    di("bov", (128, NCH))
    di("in_c", (128, NCH))
    di("ck_kv", (128, KS, REP))
    di("ck_s", (32, KS))
    di("theta", (128, REP, M))
    di("score", (32, 1))
    di("slopes", (32, 1))
    di("nsc_re", (128, 1)); di("nsc_im", (128, 1))
    di("wre_d", (128, 1)); di("wim_d", (128, 1))
    di("pos", (1, LT))
    di("mask", (128, NCORE))
    di("halo", (128, HALO))
    out_h = nc.dram_tensor("out", [128, NCH, B, LS], F32, kind="ExternalOutput").ap()

    wmir = {nm: nc.dram_tensor("m_" + nm, [NCH, NCH, 128, 128], BF16) for nm
            in ("wf", "wi", "wc", "wo", "uf", "ui", "uc", "uo")}
    win_mir = nc.dram_tensor("m_win", [24, NCH, 128, 128], BF16)
    win_smir = nc.dram_tensor("m_wins", [D, 32], BF16)
    wout_mir = nc.dram_tensor("m_wout", [REP, NCH, 128, 128], BF16)
    inb = [nc.dram_tensor(f"xin{r}", [1, 128 * TOTC], F32) for r in range(NROUND)]
    outb = [nc.dram_tensor(f"xout{r}", [NCORE, 128 * TOTC], F32) for r in range(NROUND)]

    s_ld = Sem("ld"); s_ws = Sem("ws"); s_gx = Sem("gx"); s_cc = Sem("cc")
    s_v = Sem("v"); s_s = Sem("s"); s_t = Sem("t"); s_st = Sem("st")
    s_gi = Sem("gi"); s_gb = Sem("gb"); s_lc = Sem("lc"); s_e = Sem("e")
    s_pc = Sem("pc")
    s_w = [Sem(f"w{i}") for i in range(4)]
    sems = [s_ld, s_ws, s_gx, s_cc, s_v, s_s, s_t, s_st, s_gi, s_gb, s_lc,
            s_e, s_pc] + s_w

    ops = []

    es = contextlib.ExitStack()
    with es:
        def sbt(name, shape, dt):
            return es.enter_context(nc.sbuf_tensor(name, list(shape), dt))

        x = sbt("x_sb", (128, NCH, T), F32)
        xln = sbt("xln_sb", (128, NCH, T), BF16)
        ones128 = sbt("ones128", (128, 128), BF16)
        ones_bf = sbt("ones_bf", (128, LS), BF16)
        cst = sbt("cst", (128, 4), F32)
        bias3 = sbt("bias3_sb", (128, NCH, 3), F32)
        bov = sbt("bov_sb", (128, NCH), F32)
        in_c = sbt("inc_sb", (128, NCH), F32)
        ck_kv = sbt("ckkv_sb", (128, KS, REP), F32)
        ck_s = sbt("cks_sb", (32, KS), F32)
        theta2p = sbt("theta2p", (128, REP, M), F32)
        score = sbt("score_sb", (32, 1), F32)
        slopes = sbt("slopes_sb", (32, 1), F32)
        ysc_re = sbt("yscre", (128, 1), F32)
        ysc_im = sbt("yscim", (128, 1), F32)
        tmpc1 = sbt("tmpc1", (128, 1), F32)
        tmpc2 = sbt("tmpc2", (128, 1), F32)
        pos32 = sbt("pos32", (32, LT), F32)
        maskb = sbt("maskb", (128, NCORE), F32)
        halob = sbt("halob", (128, HALO), F32)
        slope_aff = sbt("slope_aff", (32, 1), F32)
        c_aff = sbt("c_aff", (32, 1), F32)
        sp_t = sbt("sp_t", (32, 1), F32)
        dmask = sbt("dmask", (32, 1), F32)
        rowi = sbt("rowi", (32, 1), I32)
        erep8 = sbt("erep8", (32, REP, 128), BF16)
        ei32 = sbt("ei32", (32, 128), I32)
        ej32 = sbt("ej32", (32, 128), I32)
        win_s = sbt("win_s", (128, NCH, 32), BF16)


        af32 = sbt("af32", (128, 9280), F32)
        abf = sbt("abf", (128, 17920), BF16)

        cstate = af32[:, 0:4128].rearrange("p (c w) -> p c w", c=NCH)
        gate = af32[:, 4128:5676].rearrange("p (c w) -> p c w", c=3)
        gtmp = af32[:, 5676:6192]
        cbf = abf[:, 0:4128].rearrange("p (c w) -> p c w", c=NCH)
        wring = [sbt(f"wring{i}", (128, NCH, 128), BF16) for i in range(4)]

        kvb = af32[:, 0:4096].rearrange("p (r l) -> p r l", r=REP)
        tfp = [af32[:, 4096:4608], af32[:, 8256:8768]]
        fap = [af32[:, 5120:5632], af32[:, 8768:9280]]
        tfb = tfp[0]
        fab = fap[0]
        orim = af32[:, 4608:5120].bitcast(BF16)
        scomp = af32[0:32, 5632:6148]
        logtw = af32[0:32, 6192:6704]
        pwc = af32[0:32, 6704:7216]
        denc = af32[0:32, 7216:7728]
        invc = af32[0:32, 7728:8240]
        ei32 = af32[0:32, 4096:4224].bitcast(I32)
        ej32 = af32[0:32, 4224:4352].bitcast(I32)
        lnmu = af32[:, 6704:6962]
        lnu = af32[:, 6962:7220]
        lns = af32[:, 7220:7478]
        lnr = af32[:, 7478:7736]
        lnc = af32[:, 7736:7994]
        fcos = abf[:, 0:4096].rearrange("p (m l) -> p m l", m=M)
        fsin = abf[:, 4096:8192].rearrange("p (m l) -> p m l", m=M)
        s1p = [abf[:, 8192:8704], abf[:, 16896:17408]]
        s2p = [abf[:, 8704:9216], abf[:, 17408:17920]]
        s1b = s1p[0]
        s2b = s2p[0]
        s1qb = abf[:, 9216:9728]
        sinvb = abf[:, 9728:10240]
        cosvb = abf[:, 10240:10752]
        yb = abf[:, 10752:14848].rearrange("p (r l) -> p r l", r=REP)
        pwsl = abf[:, 14848:15360]
        invhi = abf[0:32, 15360:15872]
        invlo = abf[0:32, 15872:16384]
        pwbf = abf[0:32, 16384:16896]
        qb = sbt("qb", (128, 2 * M, LS), BF16)
        tots = [sbt(f"tots{i}", (128, TOTC), F32) for i in range(RING)]
        recvb = [sbt(f"recv{i}", (128, NCORE, TOTC), F32) for i in range(RING)]
        carryb = [sbt(f"carry{i}", (128, TOTC), F32) for i in range(RING)]
        qb_fl = qb[:].rearrange("p a l -> p (a l)")
        xbf = qb_fl[:, 0:NCH * 258].rearrange("p (c n) -> p c n", c=NCH)
        xsq = qb_fl[:, NCH * 258:2 * NCH * 258].rearrange("p (c n) -> p c n", c=NCH)
        cast_st = abf[:, 8192:12288]
        zch = abf[:, 4096:5128].bitcast(F32)
        kip = [fap[0].bitcast(I32), fap[1].bitcast(I32)]
        nreb = sinvb
        nimb = cosvb
        pab = sinvb
        pbb = cosvb
        ycb = abf[:, 4096:5120].bitcast(F32)
        fab2 = abf[:, 5120:6144].bitcast(F32)
        scv = denc

        psb = [es.enter_context(nc.psum_tensor(f"psb{i}", [128, 512], F32))
               for i in range(8)]

        oreb = orim[:, 0:512]
        oimb = orim[:, 512:1024]

        for s in sems:
            s.h = es.enter_context(nc.semaphore(s.name))

        nc.const_aps.register(F32, PI / 2, cst[:, 0:1])
        nc.const_aps.register(F32, -PI, cst[:, 1:2])
        nc.const_aps.register(F32, PI, cst[:, 2:3])

        blk = es.enter_context(nc.Block())

        def addop(engname, sem, k, fns, waits):
            wt = tuple((s.h, v) for (s, v) in waits if v > 0)
            if k is None:
                tgt = sem.n + len(fns)
                sem.n = tgt
                def run(e, fns=tuple(fns), wt=wt, sem=sem):
                    for h, v in wt:
                        e.wait_ge(h, v)
                    for f in fns:
                        f(e).then_inc(sem.h, 1)
            else:
                tgt = sem.n + k
                sem.n = tgt
                def run(e, fn=fns, wt=wt, sem=sem, k=k):
                    for h, v in wt:
                        e.wait_ge(h, v)
                    fn(e).then_inc(sem.h, k)
            ops.append((engname, run))
            return tgt

        vlast = [0]
        def vb(fns, waits=()):
            w = tuple(waits) + ((s_v, vlast[0]),)
            tgt = addop("vector", s_v, None, fns if isinstance(fns, list) else [fns], w)
            vlast[0] = tgt
            return tgt

        def sb_(fns, waits=()):
            return addop("scalar", s_s, None,
                         fns if isinstance(fns, list) else [fns], waits)

        def seb_(fn, waits=()):
            return addop("scalar", s_e, None, [fn], waits)

        def tb(fn, waits=()):
            return addop("tensor", s_t, 1, fn, waits)

        def syncdma(dst, src, waits=(), sem=None):
            sem = sem or s_ld
            tgt = sem.n + 16
            sem.n = tgt
            wt = tuple((s.h, v) for (s, v) in waits if v > 0)
            def run(e, dst=dst, src=src, wt=wt, sem=sem):
                for h, v in wt:
                    e.wait_ge(h, v)
                e.dma_start(out=dst, in_=src).then_inc(sem.h, 16)
            ops.append(("sync", run))
            return tgt

        def gpdma(dst, src, waits=(), sem=None):
            sem = sem or s_ws
            tgt = sem.n + 16
            sem.n = tgt
            wt = tuple((s.h, v) for (s, v) in waits if v > 0)
            def run(e, dst=dst, src=src, wt=wt, sem=sem):
                for h, v in wt:
                    e.wait_ge(h, v)
                e.dma_start(out=dst, in_=src).then_inc(sem.h, 16)
            ops.append(("gpsimd", run))
            return tgt

        def gop(fns, waits=(), sem=None):
            sem = sem or s_gi
            return addop("gpsimd", sem, None,
                         fns if isinstance(fns, list) else [fns], waits)

        def gcc(fn, waits=()):
            return addop("gpsimd", s_cc, 1, fn, waits)

        def syncwait(waits):
            wt = tuple((s.h, v) for (s, v) in waits if v > 0)
            def run(e, wt=wt):
                for h, v in wt:
                    e.wait_ge(h, v)
            ops.append(("sync", run))

        PRE = {}
        def precast4(dst_ap, src_ap, n0, sem, bar=0):
            for o0 in range(0, n0, 2):
                gpdma(dst_ap[o0:o0 + 2], src_ap[o0:o0 + 2],
                      waits=[(sem, bar)], sem=sem)
        for nm in ("wf", "uf", "wi", "ui", "wc", "uc"):
            precast4(wmir[nm].ap(), dr[nm], NCH, s_pc)
        for nm in ("wf", "uf", "wi", "ui", "wc", "uc"):
            PRE[nm] = (s_pc, s_pc.n)
        bar1 = s_pc.n
        precast4(wmir["wo"].ap(), dr["wo"], NCH, s_pc, bar1)
        precast4(wmir["uo"].ap(), dr["uo"], NCH, s_pc, bar1)
        PRE["wo"] = PRE["uo"] = (s_pc, s_pc.n)
        bar2 = s_pc.n
        precast4(wout_mir.ap(), dr["w_out"], REP, s_pc, bar2)
        PRE["w_out"] = (s_pc, s_pc.n)
        precast4(win_mir.ap(), dr["w_in"], 24, s_ws)
        gpdma(win_smir.ap(), dr["w_in_s"], waits=[], sem=s_ws)
        PRE["w_in"] = (s_ws, s_ws.n)
        t_wins = gpdma(win_s[:], win_smir.ap()
                       .rearrange("(c p) n -> p c n", p=128),
                       waits=[(s_ws, s_ws.n)])
        PRE_ALL = s_ws.n

        syncdma(x[:], dr["x"])
        for nm, dst in [("bias3", bias3), ("bov", bov), ("in_c", in_c),
                        ("ck_kv", ck_kv), ("ck_s", ck_s), ("score", score),
                        ("slopes", slopes), ("mask", maskb), ("halo", halob),
                        ("theta", theta2p)]:
            syncdma(dst[:], dr[nm])
        syncdma(pos32[:], dr["pos"].broadcast_to([32, LT]))
        syncdma(ysc_re[:], dr["nsc_re"])
        syncdma(ysc_im[:], dr["nsc_im"])
        syncdma(tmpc1[:], dr["wre_d"])
        syncdma(tmpc2[:], dr["wim_d"])
        LD0 = s_ld.n

        V_CONST = vb([lambda e: e.memset(ones128[:], 1.0),
                      lambda e: e.memset(ones_bf[:], 1.0),
                      lambda e: e.memset(cst[:, 0:1], PI / 2),
                      lambda e: e.memset(cst[:, 1:2], -PI),
                      lambda e: e.memset(cst[:, 2:3], PI)]
                     + [(lambda e, i=i: e.memset(tots[i][:], 0.0)) for i in range(RING)])

        G_IOTA = gop([lambda e: e.iota(ei32[:], pattern=[[8, 4], [0, 32]], base=0,
                                       channel_multiplier=0),
                      lambda e: e.iota(ej32[:], pattern=[[0, 128]], base=0,
                                      channel_multiplier=1)])

        V_PRE = vb([lambda e: e.tensor_scalar(theta2p[:], theta2p[:],
                                              float(1.0 / (2 * PI)), None, ALU.mult),
                    lambda e: e.tensor_tensor(ysc_re[:], ysc_re[:], tmpc1[:], ALU.mult),
                    lambda e: e.tensor_tensor(ysc_im[:], ysc_im[:], tmpc2[:], ALU.mult)],
                   waits=[(s_ld, LD0)])
        for r in range(REP):
            vb(lambda e, r=r: e.tensor_scalar(lnu[0:32, 0:128], ei32[:], r, None,
                                              ALU.add), waits=[(s_gi, G_IOTA)])
            vb(lambda e, r=r: e.tensor_tensor(erep8[:, r, :], lnu[0:32, 0:128],
                                              ej32[:], ALU.is_equal))
        G_ROWI = gop(lambda e: e.iota(rowi[:], pattern=[[0, 1]], base=0,
                                      channel_multiplier=1))
        S_AB = sb_(lambda e: e.activation(tmpc1[0:32, :], slopes[:], AF.Abs),
                   waits=[(s_ld, LD0), (s_v, V_PRE)])
        S_EX = sb_(lambda e: e.activation(tmpc1[0:32, :], tmpc1[0:32, :], AF.Exp,
                                          scale=-1.0), waits=[(s_s, S_AB)])
        V_A1 = vb(lambda e: e.tensor_scalar(tmpc2[0:32, :], tmpc1[0:32, :], 1.0,
                                            None, ALU.add), waits=[(s_s, S_EX)])
        S_LN = sb_(lambda e: e.activation(tmpc1[0:32, :], tmpc2[0:32, :], AF.Ln),
                   waits=[(s_v, V_A1)])
        V_A2 = vb(lambda e: e.tensor_scalar(tmpc2[0:32, :], slopes[:], 0.0,
                                            None, ALU.max), waits=[(s_s, S_LN)])
        V_SP = vb(lambda e: e.tensor_tensor(sp_t[:], tmpc1[0:32, :],
                                            tmpc2[0:32, :], ALU.add),
                  waits=[(s_s, S_LN)])
        vb(lambda e: e.tensor_scalar(dmask[:], rowi[:], 28, None, ALU.is_lt),
           waits=[(s_gi, G_ROWI)])
        vb(lambda e: e.tensor_scalar(lnu[0:32, 0:1], dmask[:], 2.0, -1.0,
                                     ALU.mult, ALU.add), waits=[(s_v, V_SP)])
        vb([lambda e: e.tensor_tensor(slope_aff[:], sp_t[:], lnu[0:32, 0:1], ALU.mult),
            lambda e: e.tensor_tensor(c_aff[:], sp_t[:], dmask[:], ALU.mult)])
        V_AFF = vb(lambda e: e.tensor_scalar(c_aff[:], c_aff[:], -float(L - 1),
                                             None, ALU.mult))

        bankg = [[] for _ in range(8)]
        slotg = [[] for _ in range(4)]

        lnbusy = [[(s_t, 0), (s_s, 0)]]
        def emit_ln(off, wd, tile, xw_extra=(), xgate=None):
            busy = list(lnbusy[0]) + [(s_v, vlast[0])]
            last = 0
            first_write = True
            for it in range(wd // tile):
                o = off + it * tile
                xg = [(s_ld, LD0)] if xgate else []
                t_c = vb([(lambda e, ch=ch, o=o, tile=tile: e.tensor_copy(
                    xbf[:, ch, 0:tile], x[:, ch, o:o + tile]))
                    for ch in range(NCH)],
                    waits=tuple(busy) + tuple(xg))
                S_SQ = sb_([(lambda e, ch=ch, tile=tile: e.activation(
                    xsq[:, ch, 0:tile], xbf[:, ch, 0:tile], AF.Square))
                    for ch in range(NCH)],
                    waits=[(s_v, t_c)] + busy)
                def statmm(e, tile=tile):
                    for ch in range(NCH):
                        e.matmul(psb[6][:, 0:tile], ones128[:], xbf[:, ch, 0:tile],
                                 start=(ch == 0), stop=(ch == NCH - 1))
                    for ch in range(NCH):
                        i1 = e.matmul(psb[7][:, 0:tile], ones128[:], xsq[:, ch, 0:tile],
                                      start=(ch == 0), stop=(ch == NCH - 1))
                    return i1
                T_STAT = tb(statmm, waits=[(s_v, t_c), (s_s, S_SQ), (s_v, V_CONST)]
                            + bankg[6] + bankg[7])
                v1 = vb([lambda e, tile=tile: e.tensor_scalar(
                            lnmu[:, 0:tile], psb[6][:, 0:tile], float(1.0 / D),
                            None, ALU.mult),
                         lambda e, tile=tile: e.tensor_scalar(
                            lnu[:, 0:tile], psb[7][:, 0:tile], float(1.0 / D),
                            1e-5, ALU.mult, ALU.add)],
                        waits=[(s_t, T_STAT)])
                bankg[6] = [(s_v, v1)]
                bankg[7] = [(s_v, v1)]
                vb(lambda e, tile=tile: e.tensor_tensor(
                    lnc[:, 0:tile], lnmu[:, 0:tile], lnmu[:, 0:tile], ALU.mult))
                v3 = vb(lambda e, tile=tile: e.tensor_tensor(
                    lnu[:, 0:tile], lnu[:, 0:tile], lnc[:, 0:tile], ALU.subtract))
                s4 = sb_(lambda e, tile=tile: e.activation(
                    lns[:, 0:tile], lnu[:, 0:tile], AF.Ln), waits=[(s_v, v3)])
                s5 = sb_(lambda e, tile=tile: e.activation(
                    lnr[:, 0:tile], lns[:, 0:tile], AF.Exp, scale=-0.5),
                    waits=[(s_s, s4)])
                vb(lambda e: e.memset(cst[:, 3:4], 0.0), waits=[(s_s, s5)])
                for ch in range(NCH):
                    xw = tuple(xw_extra) if first_write else ()
                    first_write = False
                    vb(lambda e, ch=ch, o=o, tile=tile: e.tensor_tensor(
                        lnc[:, 0:tile], x[:, ch, o:o + tile], lnmu[:, 0:tile],
                        ALU.subtract), waits=xw)
                    last = vb(lambda e, ch=ch, o=o, tile=tile: e.tensor_tensor(
                        xln[:, ch, o:o + tile], lnc[:, 0:tile], lnr[:, 0:tile],
                        ALU.mult))
                busy = [(s_t, T_STAT), (s_s, S_SQ)]
            lnbusy[0] = busy
            return last

        slot_i = [0]
        def slotdma(sl, src, waits):
            sem = s_w[sl]
            tgt = sem.n + 16
            sem.n = tgt
            wt = tuple((s.h, v) for (s, v) in waits if v > 0)
            def run(e, src=src, wt=wt, sem=sem, sl=sl):
                for h, v in wt:
                    e.wait_ge(h, v)
                e.dma_start(out=wring[sl][:], in_=src).then_inc(sem.h, 16)
            ops.append(("sync", run))
            return tgt

        def stream_w(nm, och):
            sl = slot_i[0] % 4
            slot_i[0] += 1
            tg = slotdma(sl, wmir[nm].ap()[och].rearrange("c p n -> p c n"),
                         waits=[PRE[nm]] + slotg[sl])
            return sl, tg

        def stream_piece(idx):
            sl = slot_i[0] % 4
            slot_i[0] += 1
            tg = slotdma(sl, win_mir.ap()[idx].rearrange("c p n -> p c n"),
                         waits=[PRE["w_in"]] + slotg[sl])
            return sl, tg

        def emit_ffn(regions, tile, ln_tgt):
            Wd0 = regions[0][1]
            S_C0 = sb_([(lambda e, ch=ch, Wd0=Wd0: e.activation(
                cstate[:, ch, 0:Wd0], x[:, 0, 0:Wd0], AF.Identity,
                bias=in_c[:, ch:ch + 1], scale=0.0)) for ch in range(NCH)],
                waits=[(s_ld, LD0), (s_v, vlast[0]), (s_t, s_t.n)])
            V_CB = vb([(lambda e, ch=ch, Wd0=Wd0: e.tensor_copy(
                cbf[:, ch, 0:Wd0], cstate[:, ch, 0:Wd0])) for ch in range(NCH)],
                waits=[(s_s, S_C0)])
            gnames = ("wf", "wi", "wc")
            unames = ("uf", "ui", "uc")
            prev_cb = V_CB
            for t_i, (xo, Wd) in enumerate(regions):
                nt = Wd // tile
                for och in range(NCH):
                    act_waits = []
                    for g in range(3):
                        pw_ = stream_w(gnames[g], och)
                        pu_ = stream_w(unames[g], och)
                        banks = [g * 2 + it for it in range(nt)]
                        bgw = []
                        for bk in banks:
                            bgw += bankg[bk]
                        def gmm(e, xo=xo, Wd=Wd, tile=tile,
                                pw_=pw_, pu_=pu_, banks=tuple(banks), nt=nt):
                            for it in range(nt):
                                no = it * tile
                                for ich in range(NCH):
                                    e.matmul(
                                        psb[banks[it]][:, 0:tile],
                                        wring[pw_[0]][:, ich, :],
                                        xln[:, ich, xo + no:xo + no + tile],
                                        start=(ich == 0), stop=False)
                                for ich in range(NCH):
                                    i1 = e.matmul(
                                        psb[banks[it]][:, 0:tile],
                                        wring[pu_[0]][:, ich, :],
                                        cbf[:, ich, no:no + tile],
                                        start=False, stop=(ich == NCH - 1))
                            return i1
                        T_G = tb(gmm, waits=[(s_w[pw_[0]], pw_[1]),
                                             (s_w[pu_[0]], pu_[1]),
                                             (s_v, prev_cb), (s_v, ln_tgt)]
                                 + bgw)
                        slotg[pw_[0]] = [(s_t, T_G)]
                        slotg[pu_[0]] = [(s_t, T_G)]
                        S_A = sb_([(lambda e, g=g, och=och, bank=banks[it2],
                                    no=it2 * tile, tile=tile: e.activation(
                                        gate[:, g, no:no + tile],
                                        psb[bank][:, 0:tile],
                                        AF.Tanh if g == 2 else AF.Sigmoid,
                                        bias=bias3[:, och, g:g + 1]))
                                   for it2 in range(nt)],
                                  waits=[(s_t, T_G), (s_v, vlast[0])])
                        for bk in banks:
                            bankg[bk] = [(s_s, S_A)]
                        act_waits.append((s_s, S_A))
                    vb(lambda e, Wd=Wd: e.tensor_tensor(
                        gtmp[:, 0:Wd], gate[:, 1, 0:Wd], gate[:, 2, 0:Wd],
                        ALU.mult), waits=act_waits)
                    vb(lambda e, och=och, Wd=Wd: e.tensor_tensor(
                        gate[:, 0, 0:Wd], gate[:, 0, 0:Wd],
                        cstate[:, och, 0:Wd], ALU.mult))
                    vb(lambda e, och=och, Wd=Wd: e.tensor_tensor(
                        cstate[:, och, 0:Wd], gtmp[:, 0:Wd], gate[:, 0, 0:Wd],
                        ALU.add))
                prev_cb = vb([(lambda e, ch=ch, Wd=Wd: e.tensor_copy(
                    cbf[:, ch, 0:Wd], cstate[:, ch, 0:Wd])) for ch in range(NCH)],
                    waits=[(s_t, s_t.n)])
            for t_i, (xo, Wd) in enumerate(regions):
                nt = Wd // tile
                for och in range(NCH):
                    slw, tgw = stream_w("wo", och)
                    slu, tgu = stream_w("uo", och)
                    T_LAST = 0
                    for it in range(nt):
                        no = it * tile
                        bank = 6 + (it % 2)
                        def hmm(e, slw=slw, slu=slu, no=no, xo=xo,
                                tile=tile, bank=bank):
                            for ich in range(NCH):
                                e.matmul(
                                    psb[bank][:, 0:tile],
                                    wring[slw][:, ich, :],
                                    xln[:, ich, xo + no:xo + no + tile],
                                    start=(ich == 0), stop=False)
                            for ich in range(NCH):
                                i1 = e.matmul(
                                    psb[bank][:, 0:tile],
                                    wring[slu][:, ich, :],
                                    cbf[:, ich, no:no + tile],
                                    start=False, stop=(ich == NCH - 1))
                            return i1
                        T_H = tb(hmm, waits=[(s_w[slw], tgw), (s_w[slu], tgu),
                                             (s_v, prev_cb)] + bankg[bank])
                        T_LAST = T_H
                        S_H = sb_(lambda e, och=och, bank=bank, no=no, tile=tile:
                                  e.activation(gtmp[:, no:no + tile],
                                               psb[bank][:, 0:tile], AF.Tanh,
                                               bias=bov[:, och:och + 1]),
                                  waits=[(s_t, T_H), (s_v, vlast[0])])
                        bankg[bank] = [(s_s, S_H)]
                        vb(lambda e, och=och, no=no, xo=xo, tile=tile:
                           e.scalar_tensor_tensor(
                               x[:, och, xo + no:xo + no + tile],
                               gtmp[:, no:no + tile], 0.5,
                               x[:, och, xo + no:xo + no + tile],
                               ALU.mult, ALU.add),
                           waits=[(s_s, S_H)])
                    slotg[slw] = [(s_t, T_LAST)]
                    slotg[slu] = [(s_t, T_LAST)]
            return vlast[0]

        lnA = emit_ln(0, T, 258, xw_extra=[(s_v, V_AFF)], xgate=True)

        ffnA = emit_ffn([(ti * LT, LT) for ti in range(B)], 258, lnA)

        rnd = [0]
        feat_prev = [0]
        XCH_BOUNCE = []
        MASKS = []
        lnb_list = []
        pend_wout = [None]
        pend_se = [None]
        sy_prev = [0]
        for b in range(B):
            lnb_list.append(emit_ln(b * LT, LT, 258, xw_extra=[(s_t, s_t.n)]))
        for b in range(B):
            xo = b * LT
            lnb = lnb_list[b]

            if pend_se[0] is not None:
                pend_se[0]()
                pend_se[0] = None

            for r in range(REP):
                sl, tg = stream_piece(r)
                evac = []
                T_KV2 = 0
                for it in range(2):
                    no = it * 258
                    bank = 5 + it
                    def kvmm(e, sl=sl, no=no, xo=xo, bank=bank):
                        for ich in range(NCH):
                            i1 = e.matmul(psb[bank][:, 0:258],
                                          wring[sl][:, ich, :],
                                          xln[:, ich, xo + no:xo + no + 258],
                                          start=(ich == 0), stop=(ich == NCH - 1))
                        return i1
                    T_KV = tb(kvmm, waits=[(s_w[sl], tg), (s_v, lnb)] + bankg[bank])
                    T_KV2 = T_KV
                    S_E = sb_(lambda e, no=no, bank=bank: e.activation(
                        zch[:, no:no + 258], psb[bank][:, 0:258], AF.Identity),
                        waits=[(s_t, T_KV), (s_v, vlast[0]),
                               (s_e, sy_prev[0])])
                    bankg[bank] = [(s_s, S_E)]
                    evac.append((s_s, S_E))
                slotg[sl] = [(s_t, T_KV2)]
                vb(lambda e: e.tensor_tensor(zch[:, 0:HALO], zch[:, 0:HALO],
                                             halob[:], ALU.mult), waits=evac)
                vb(lambda e, r=r: e.tensor_scalar(kvb[:, r, :], zch[:, 1:1 + LS],
                                                  ck_kv[:, 0, r:r + 1], None,
                                                  ALU.mult))
                for j in range(1, KS):
                    vb(lambda e, r=r, j=j: e.scalar_tensor_tensor(
                        kvb[:, r, :], zch[:, 1 + j:1 + j + LS],
                        ck_kv[:, j, r:r + 1], kvb[:, r, :], ALU.mult, ALU.add))
            KVD = vlast[0]

            if pend_wout[0] is not None:
                pend_wout[0]()
                pend_wout[0] = None

            evac = []
            for it in range(2):
                no = it * 258
                bank = 5 + it
                def smm(e, no=no, xo=xo, bank=bank):
                    for ich in range(NCH):
                        i1 = e.matmul(psb[bank][0:32, 0:258], win_s[:, ich, :],
                                      xln[:, ich, xo + no:xo + no + 258],
                                      start=(ich == 0), stop=(ich == NCH - 1))
                    return i1
                T_S = tb(smm, waits=[(s_ws, t_wins), (s_v, lnb)] + bankg[bank])
                S_E = sb_(lambda e, no=no, bank=bank: e.activation(
                    scomp[:, no:no + 258], psb[bank][0:32, 0:258], AF.Identity),
                    waits=[(s_t, T_S), (s_v, vlast[0])])
                bankg[bank] = [(s_s, S_E)]
                evac.append((s_s, S_E))
            vb(lambda e: e.tensor_tensor(scomp[:, 0:HALO], scomp[:, 0:HALO],
                                         halob[0:32, :], ALU.mult), waits=evac)
            vb(lambda e: e.tensor_scalar(scv, scomp[:, 1:1 + LS], ck_s[:, 0:1],
                                         None, ALU.mult))
            for j in range(1, KS):
                vb(lambda e, j=j: e.scalar_tensor_tensor(
                    scv, scomp[:, 1 + j:1 + j + LS], ck_s[:, j:j + 1], scv,
                    ALU.mult, ALU.add))
            vb(lambda e: e.tensor_scalar(logtw, pos32[:, HALO:HALO + LS],
                                         slope_aff[:], c_aff[:], ALU.mult, ALU.add),
               waits=[(s_ld, LD0)])
            vb(lambda e: e.tensor_scalar(pwc, scv, score[:], 20.0,
                                         ALU.mult, ALU.min))
            V_P3 = vb(lambda e: e.scalar_tensor_tensor(pwc, pwc, -20.0,
                                                       logtw, ALU.max, ALU.add))
            tot_guard = ([(s_gx, XCH_BOUNCE[b * REP - RING])]
                         if b * REP >= RING else [])
            S_PW = sb_(lambda e, b=b: e.activation(
                pwc, pwc, AF.Exp,
                accum_out=tots[(b * REP) % RING][0:32, 16:17]),
                waits=[(s_v, V_P3)] + tot_guard)
            S_PWB = sb_(lambda e: e.activation(pwbf, pwc, AF.Identity),
                        waits=[(s_s, S_PW), (s_t, s_t.n)])

            Q_TGTS = []
            def emit_qproj():
                for j in range(2 * M):
                    sl, tg = stream_piece(8 + j)
                    bank = 7
                    def qmm(e, sl=sl, xo=xo, bank=bank):
                        for ich in range(NCH):
                            i1 = e.matmul(psb[bank][:, 0:512],
                                          wring[sl][:, ich, :],
                                          xln[:, ich, xo + HALO:xo + HALO + LS],
                                          start=(ich == 0), stop=(ich == NCH - 1))
                        return i1
                    T_Q = tb(qmm, waits=[(s_w[sl], tg), (s_v, lnb)] + bankg[bank])
                    slotg[sl] = [(s_t, T_Q)]
                    S_Q = sb_(lambda e, j=j, bank=bank: e.activation(
                        qb[:, j, :], psb[bank][:, 0:512], AF.Identity),
                        waits=[(s_t, T_Q), (s_v, vlast[0])])
                    bankg[bank] = [(s_s, S_Q)]
                    Q_TGTS.append((s_s, S_Q))

            def slice_phaseA(rep):
                r_id = rnd[0]; rnd[0] += 1
                rr = r_id % RING
                tot = tots[rr]; rcv = recvb[rr]; car = carryb[rr]
                tot_guard = ([(s_gx, XCH_BOUNCE[r_id - RING])]
                             if r_id >= RING else [])

                def pwrep_mm(e, rep=rep):
                    return e.matmul(psb[5][:, 0:512], erep8[:, rep, :], pwbf,
                                    start=True, stop=True)
                T_PW = tb(pwrep_mm, waits=[(s_s, S_PWB), (s_v, V_PRE)] + bankg[5])
                S_PWS = sb_(lambda e: e.activation(pwsl, psb[5][:, 0:512],
                                                   AF.Identity),
                            waits=[(s_t, T_PW), (s_v, feat_prev[0])])
                bankg[5] = [(s_s, S_PWS)]

                VF_t = {}; K_t = {}; S1_t = {}; S2_t = {}; FA_t = {}; SINV_t = {}
                for mm in range(M + 2):
                    if mm < M:
                        m = mm; p = m % 2
                        w = []
                        if m == 0:
                            w = [(s_v, KVD), (s_s, S_PWS), (s_ws, PRE_ALL)]
                        if m >= 2:
                            w = w + [(s_v, VF_t[m - 2]), (s_s, S1_t[m - 2])]
                        T1 = sb_(lambda e, rep=rep, m=m, p=p: e.activation(
                            tfp[p], kvb[:, rep, :], AF.Identity,
                            scale=theta2p[:, rep, m:m + 1]), waits=w)
                        w2 = [(s_s, T1)]
                        if m >= 2:
                            w2 = w2 + [(s_s, S2_t[m - 2])]
                        K_t[m] = sb_(lambda e, p=p: e.activation(
                            kip[p], tfp[p], AF.Identity), waits=w2)
                    if mm == 1 and pend_se[0] is not None:
                        pend_se[0]()
                        pend_se[0] = None
                    if 0 <= mm - 1 < M:
                        m = mm - 1; p = m % 2
                        VF_t[m] = vb(lambda e, p=p: e.tensor_tensor(
                            tfp[p], tfp[p], kip[p], ALU.subtract),
                            waits=[(s_s, K_t[m])])
                        FA_t[m] = sb_(lambda e, p=p: e.activation(
                            fap[p], tfp[p], AF.Abs), waits=[(s_v, VF_t[m])])
                        w3 = []
                        if m >= 2:
                            w3 = [(s_v, SINV_t[m - 2])]
                        S1_t[m] = sb_(lambda e, p=p: e.activation(
                            s1p[p], tfp[p], AF.Sin, scale=PI), waits=w3)
                        S2_t[m] = sb_(lambda e, p=p: e.activation(
                            s2p[p], fap[p], AF.Sin, bias=PI / 2, scale=-PI),
                            waits=[(s_s, FA_t[m])])
                    if 0 <= mm - 2 < M:
                        m = mm - 2; p = m % 2
                        w4 = [(s_s, S1_t[m])]
                        if m == 0:
                            w4 = w4 + tot_guard
                        vb(lambda e, p=p: e.tensor_tensor(
                            s1qb, s1p[p], s1p[p], ALU.mult), waits=w4)
                        SINV_t[m] = vb(lambda e, p=p: e.tensor_tensor(
                            sinvb, s1p[p], s2p[p], ALU.mult),
                            waits=[(s_s, S2_t[m])])
                        vb(lambda e: e.tensor_scalar(cosvb, s1qb, -2.0, 1.0,
                                                     ALU.mult, ALU.add))
                        wfs = [(s_e, sy_prev[0])] if m < 2 else []
                        vb(lambda e, m=m, tot=tot: e.scalar_tensor_tensor(
                            fsin[:, m, :], sinvb, 2.0, pwsl, ALU.mult, ALU.mult,
                            accum_out=tot[:, 8 + m:9 + m]), waits=wfs)
                        vb(lambda e, m=m, tot=tot: e.scalar_tensor_tensor(
                            fcos[:, m, :], cosvb, 1.0, pwsl, ALU.mult, ALU.mult,
                            accum_out=tot[:, m:m + 1]))
                FEAT = vlast[0]
                feat_prev[0] = FEAT

                bounce = syncdma(inb[r_id].ap(), tot[:],
                                 waits=[(s_v, FEAT), (s_s, S_PW)], sem=s_gx)
                XCH_BOUNCE.append(bounce)
                CCT = gcc(lambda e, r_id=r_id: e.collective_compute(
                    "AllGather", ALU.bypass,
                    replica_groups=[list(range(NCORE))],
                    ins=[inb[r_id].ap().opt()],
                    outs=[outb[r_id].ap().opt()]),
                    waits=[(s_gx, bounce)])
                rb_guard = [MASKS[r_id - RING]] if r_id >= RING else []
                readback = syncdma(rcv[:],
                                   outb[r_id].ap().rearrange("s (p c) -> p s c", p=128),
                                   waits=[(s_cc, CCT)] + rb_guard, sem=s_gb)

                for m in range(M):
                    vb([lambda e, m=m: e.tensor_tensor_scan(
                            fcos[:, m, :], ones_bf[:], fcos[:, m, :], 0.0,
                            ALU.mult, ALU.add),
                        lambda e, m=m: e.tensor_tensor_scan(
                            fsin[:, m, :], ones_bf[:], fsin[:, m, :], 0.0,
                            ALU.mult, ALU.add)])
                if rep == 0:
                    vb(lambda e: e.tensor_tensor_scan(
                        denc, ones_bf[0:32, :], pwc, 0.0, ALU.mult, ALU.add))
                return dict(rep=rep, r_id=r_id, rcv=rcv, car=car,
                            readback=readback)

            def slice_phaseB(ctx):
                rep = ctx["rep"]; r_id = ctx["r_id"]
                rcv = ctx["rcv"]; car = ctx["car"]
                readback = ctx["readback"]

                vb(lambda e, car=car, rcv=rcv: e.tensor_scalar(
                    car[:], rcv[:, 0, :], maskb[:, 0:1], None, ALU.mult),
                   waits=[(s_gb, readback)])
                for jc in range(1, NCORE):
                    vb(lambda e, jc=jc, car=car, rcv=rcv: e.scalar_tensor_tensor(
                        car[:], rcv[:, jc, :], maskb[:, jc:jc + 1], car[:],
                        ALU.mult, ALU.add))
                VMASK = vlast[0]
                MASKS.append((s_v, VMASK))

                CAD = []
                for m in range(M):
                    wca = [(s_v, VMASK)] if m == 0 else []
                    sb_(lambda e, m=m, car=car: e.activation(
                        fcos[:, m, :], fcos[:, m, :], AF.Identity,
                        bias=car[:, m:m + 1]), waits=wca)
                    CAD.append(sb_(lambda e, m=m, car=car: e.activation(
                        fsin[:, m, :], fsin[:, m, :], AF.Identity,
                        bias=car[:, 8 + m:9 + m])))
                if rep == 0:
                    vb(lambda e, car=car: e.tensor_scalar(
                        denc, denc, car[0:32, 16:17], None, ALU.add))
                    vb(lambda e: e.tensor_scalar(invc, denc, 1e-4, None,
                                                 ALU.max))
                    vb(lambda e: e.reciprocal(denc, invc))
                    vb(lambda e: e.tensor_copy(invhi, denc))
                    vb(lambda e: e.tensor_tensor(invlo, denc, invhi,
                                                 ALU.subtract))
                INVD = vlast[0]

                PROD3 = 0
                for m in range(M):
                    vb([lambda e, m=m: e.tensor_tensor(pab, fcos[:, m, :],
                                                       qb[:, 2 * m, :], ALU.mult),
                        lambda e, m=m: e.tensor_tensor(pbb, fsin[:, m, :],
                                                       qb[:, 2 * m + 1, :], ALU.mult)],
                       waits=[Q_TGTS[2 * m + 1], (s_s, CAD[m])])
                    if m == 0:
                        vb(lambda e: e.tensor_scalar(oreb, pab, 1.0, None,
                                                     ALU.mult))
                    else:
                        vb(lambda e: e.tensor_tensor(oreb, oreb, pab,
                                                     ALU.add))
                    vb(lambda e: e.tensor_tensor(oreb, oreb, pbb, ALU.add))
                    vb([lambda e, m=m: e.tensor_tensor(pab, fsin[:, m, :],
                                                       qb[:, 2 * m, :], ALU.mult),
                        lambda e, m=m: e.tensor_tensor(pbb, fcos[:, m, :],
                                                       qb[:, 2 * m + 1, :], ALU.mult)])
                    if m == 3:
                        PROD3 = vlast[0]
                    if m == 0:
                        vb(lambda e: e.tensor_scalar(oimb, pab, 1.0, None,
                                                     ALU.mult))
                    else:
                        vb(lambda e: e.tensor_tensor(oimb, oimb, pab,
                                                     ALU.add))
                    vb(lambda e: e.tensor_tensor(oimb, oimb, pbb,
                                                 ALU.subtract))
                OD = vlast[0]

                def invrep_mm(e, rep=rep):
                    e.matmul(psb[6][:, 0:512], erep8[:, rep, :], invhi,
                             start=True, stop=False)
                    return e.matmul(psb[6][:, 0:512], erep8[:, rep, :], invlo,
                                    start=False, stop=True)
                T_INV = tb(invrep_mm, waits=[(s_v, INVD)] + bankg[6])
                ir_tgt = 2 * r_id + 1
                sy_tgt = 2 * r_id + 2
                bankg[6] = [(s_e, ir_tgt)]
                yab = kvb[:, rep, :]
                vb([lambda e, yab=yab: e.tensor_tensor(yab, oreb, fab2,
                                                       ALU.mult),
                    lambda e: e.tensor_tensor(ycb, oimb, fab2, ALU.mult)],
                   waits=[(s_e, ir_tgt)])
                vb(lambda e, yab=yab: e.tensor_scalar(yab, yab, ysc_re[:],
                                                      None, ALU.mult))
                V_YE = vb(lambda e, yab=yab: e.scalar_tensor_tensor(
                    ycb, ycb, ysc_im[:], yab, ALU.mult, ALU.add))
                vb(lambda e: e.memset(lnc[:, 0:1], 0.0))
                sy_prev[0] = sy_tgt

                def emit_epi(rep=rep, T_INV=T_INV, PROD3=PROD3, V_YE=V_YE,
                             ir_tgt=ir_tgt, sy_tgt=sy_tgt):
                    t1 = seb_(lambda e: e.activation(fab2, psb[6][:, 0:512],
                                                     AF.Identity),
                              waits=[(s_t, T_INV), (s_v, PROD3)])
                    t2 = seb_(lambda e, rep=rep: e.activation(
                        yb[:, rep, :], ycb, AF.Silu), waits=[(s_v, V_YE)])
                    assert t1 == ir_tgt and t2 == sy_tgt
                pend_se[0] = emit_epi

            ctx0 = slice_phaseA(0)
            emit_qproj()
            slice_phaseB(ctx0)
            for rep in range(1, REP):
                slice_phaseB(slice_phaseA(rep))

            def make_wout(xo=xo, YD=vlast[0], sy_last=2 * rnd[0]):
                def emit_wout():
                    T_WOUT = 0
                    for r in range(REP):
                        sl = slot_i[0] % 4
                        slot_i[0] += 1
                        tg = slotdma(sl, wout_mir.ap()[r].rearrange("c p n -> p c n"),
                                     waits=[PRE["w_out"]] + slotg[sl])
                        bg = []
                        if r == 0:
                            for dch in range(NCH):
                                bg += bankg[dch]
                        def womm(e, sl=sl, r=r):
                            for dch in range(NCH):
                                i1 = e.matmul(psb[dch][:, 0:512], wring[sl][:, dch, :],
                                              yb[:, r, :], start=(r == 0), stop=(r == REP - 1))
                            return i1
                        T_WOUT = tb(womm, waits=[(s_w[sl], tg), (s_v, YD),
                                                 (s_e, sy_last)] + bg)
                        slotg[sl] = [(s_t, T_WOUT)]
                    for dch in range(NCH):
                        vt = vb(lambda e, dch=dch, xo=xo: e.scalar_tensor_tensor(
                            x[:, dch, xo + HALO:xo + HALO + LS], psb[dch][:, 0:512], 1.0,
                            x[:, dch, xo + HALO:xo + HALO + LS], ALU.mult, ALU.add),
                            waits=[(s_t, T_WOUT)])
                        bankg[dch] = [(s_v, vt)]
                return emit_wout
            pend_wout[0] = make_wout()
        pend_se[0]()
        pend_se[0] = None
        pend_wout[0]()
        pend_wout[0] = None

        lnB = 0
        for b in range(B):
            lnB = emit_ln(b * LT + HALO, LS, 256, xw_extra=[(s_t, s_t.n)])
        ffnB = emit_ffn([(ti * LT + HALO, LS) for ti in range(B)], 256, lnB)

        st_t = 0
        for ch in range(NCH):
            for b in range(B):
                tgt = s_st.n + 16
                s_st.n = tgt
                def run(e, ch=ch, b=b, w=ffnB):
                    e.wait_ge(s_v.h, w)
                    e.dma_start(out=out_h[:, ch, b, :],
                                in_=x[:, ch, b * LT + HALO:b * LT + HALO + LS]
                                ).then_inc(s_st.h, 16)
                ops.append(("sync", run))
                st_t = tgt
        syncwait([(s_st, st_t)])

        @blk.sync
        def _(e):
            for en, fn in ops:
                if en == "sync":
                    fn(e)

        @blk.gpsimd
        def _(e):
            for en, fn in ops:
                if en == "gpsimd":
                    fn(e)

        @blk.vector
        def _(e):
            for en, fn in ops:
                if en == "vector":
                    fn(e)

        @blk.scalar
        def _(e):
            for en, fn in ops:
                if en == "scalar":
                    fn(e)

        @blk.tensor
        def _(e):
            for en, fn in ops:
                if en == "tensor":
                    fn(e)

    return nc


def _stage_core(core, xf, win, win_s, wout, ffn_t, bias_t, conv_k, theta,
                slopes_cat, score_s, nsc, wre, wim):
    lo = LS * core - HALO
    xs = np.zeros((128, NCH, T), np.float32)
    for b in range(B):
        lo2 = max(lo, 0)
        seg = xf[b, lo2:LS * (core + 1), :]
        segT = np.ascontiguousarray(seg.T)
        n = seg.shape[0]
        for ch in range(NCH):
            xs[:, ch, b * LT + (LT - n):(b + 1) * LT] = segT[ch * 128:(ch + 1) * 128]
    pos = np.arange(lo, LS * (core + 1), dtype=np.float32).reshape(1, LT)
    mask = np.zeros((128, NCORE), np.float32)
    mask[:, :core] = 1.0
    halo = np.full((128, HALO), 0.0 if core == 0 else 1.0, np.float32)
    d = {"x": xs, "w_in": win, "w_in_s": win_s, "w_out": wout, "pos": pos,
         "mask": mask,
         "halo": halo, "theta": theta, "score": score_s, "slopes": slopes_cat,
         "ck_kv": conv_k[0], "ck_s": conv_k[1],
         "nsc_re": nsc[0], "nsc_im": nsc[1], "wre_d": wre, "wim_d": wim}
    d.update(ffn_t)
    d.update(bias_t)
    return d


def kernel(**inputs):
    g = {k: np.asarray(v, np.float32) for k, v in inputs.items()}
    xf = g["L0_attn_x"]
    W_in = g["L0_attn_W_in"]
    conv_k = g["L0_attn_conv_k"]
    theta_in = g["L0_attn_theta"].reshape(K, 32, M)
    nsc = g["L0_attn_norm_scale"]
    W_out = g["L0_attn_W_out"]

    pidx = np.arange(128)
    kqv, hv = pidx // 32, pidx % 32
    kv_cols = np.empty((REP, 128), np.int64)
    for r in range(REP):
        kv_cols[r] = (kqv * 8 + r) * 32 + hv
    q_cols = np.empty((2 * M, 128), np.int64)
    for m in range(M):
        for c in range(2):
            q_cols[2 * m + c] = 1056 + (pidx * 8 + m) * 2 + c
    perm = np.concatenate([kv_cols.reshape(-1), 1024 + np.arange(32),
                           q_cols.reshape(-1)])
    win = W_in[:, perm]
    win_p = np.empty((24, NCH, 128, 128), np.float32)
    for i, c0 in enumerate(list(range(0, 1024, 128)) +
                           list(range(1056, 3104, 128))):
        win_p[i] = win[:, c0:c0 + 128].reshape(NCH, 128, 128)
    win_sh = np.ascontiguousarray(win[:, 1024:1056])

    wout = np.empty((REP, 128, D), np.float32)
    for r in range(REP):
        wout[r] = W_out[(kqv * 8 + r) * 32 + hv, :]
    wout = np.ascontiguousarray(
        wout.reshape(REP, 128, NCH, 128).transpose(0, 2, 1, 3))

    ckkv = np.empty((128, KS, REP), np.float32)
    for j in range(KS):
        for r in range(REP):
            ckkv[:, j, r] = conv_k[j, 0, (kqv * 8 + r) * 32 + hv]
    cks = np.ascontiguousarray(conv_k[:, 0, 1024:1056].T)

    theta = np.empty((128, REP, M), np.float32)
    for r in range(REP):
        theta[:, r, :] = theta_in[kqv * 8 + r, hv, :]

    slopes_cat = np.concatenate([g["L0_attn_decay_slopes"],
                                 g["L0_attn_anchor_slopes"]]).reshape(32, 1)
    score_s = g["L0_attn_score_scale"].reshape(32, 1)
    nsc_re = np.ascontiguousarray(np.tile(nsc[:32], 4).reshape(128, 1))
    nsc_im = np.ascontiguousarray(np.tile(nsc[32:], 4).reshape(128, 1))
    wre_d = np.ascontiguousarray(np.tile(np.diagonal(g["L0_attn_W_re"]), 4)
                                 .reshape(128, 1))
    wim_d = np.ascontiguousarray(np.tile(np.diagonal(g["L0_attn_W_im"]), 4)
                                 .reshape(128, 1))

    ffn_t = {}
    for nm in ("wf", "wi", "wc", "wo", "uf", "ui", "uc", "uo"):
        a = g["L0_ffn_" + nm].T
        ffn_t[nm] = np.ascontiguousarray(
            a.reshape(NCH, 128, NCH, 128).transpose(2, 0, 1, 3))

    def col_layout(v):
        return np.ascontiguousarray(v.reshape(NCH, 128).T)
    bias_t = {
        "bias3": np.ascontiguousarray(np.stack(
            [col_layout(g["L0_ffn_bf"]), col_layout(g["L0_ffn_bi"]),
             col_layout(g["L0_ffn_bc"])], axis=2)),
        "bov": col_layout(g["L0_ffn_bo"]),
        "in_c": col_layout(g["L0_ffn_in_c"]),
    }

    in_maps = [_stage_core(c, xf, win_p, win_sh, wout, ffn_t, bias_t,
                           (ckkv, cks), theta, slopes_cat, score_s,
                           (nsc_re, nsc_im), wre_d, wim_d)
               for c in range(NCORE)]

    nc = build_program()
    res = run_bass_kernel_spmd(nc, in_maps, core_ids=list(range(NCORE)))

    out = np.empty((B, L, D), np.float32)
    for c in range(NCORE):
        o = res.results[c]["out"]
        o = np.transpose(o, (2, 3, 1, 0)).reshape(B, LS, D)
        out[:, LS * c:LS * (c + 1), :] = o
    return out


# version 19 (final)
# speedup vs baseline: 1.0952x; 1.0952x over previous
import contextlib
import numpy as np
import ml_dtypes

import concourse.bass as bass
import concourse.mybir as mybir
from concourse.bass_utils import run_bass_kernel_spmd

F32 = mybir.dt.float32
BF16 = mybir.dt.bfloat16
I32 = mybir.dt.int32
AF = mybir.ActivationFunctionType
ALU = mybir.AluOpType
BF = ml_dtypes.bfloat16

NCORE = 8
D = 1024
NCH = 8
K = 32
M = 8
REP = 8
KS = 4
B = 4
L = 4096
LS = L // NCORE
HALO = 4
LT = LS + HALO
T = B * LT
PI = float(np.pi)
NROUND = B * REP
TOTC = 17
RING = 2


class Sem:
    def __init__(self, name):
        self.name = name
        self.n = 0
        self.h = None


def build_program():
    nc = bass.Bass("TRN2", target_bir_lowering=False)

    dr = {}
    def di(name, shape):
        dr[name] = nc.dram_tensor(name, list(shape), F32, kind="ExternalInput").ap()

    di("x", (128, NCH, T))
    di("w_in", (24, NCH, 128, 128))
    di("w_in_s", (D, 32))
    di("w_out", (REP, NCH, 128, 128))
    for nm in ("wf", "wi", "wc", "wo", "uf", "ui", "uc", "uo"):
        di(nm, (NCH, NCH, 128, 128))
    di("bias3", (128, NCH, 3))
    di("bov", (128, NCH))
    di("in_c", (128, NCH))
    di("ck_kv", (128, KS, REP))
    di("ck_s", (32, KS))
    di("theta", (128, REP, M))
    di("score", (32, 1))
    di("slopes", (32, 1))
    di("nsc_re", (128, 1)); di("nsc_im", (128, 1))
    di("wre_d", (128, 1)); di("wim_d", (128, 1))
    di("pos", (1, LT))
    di("mask", (128, NCORE))
    di("halo", (128, HALO))
    out_h = nc.dram_tensor("out", [128, NCH, B, LS], F32, kind="ExternalOutput").ap()

    wmir = {nm: nc.dram_tensor("m_" + nm, [NCH, NCH, 128, 128], BF16) for nm
            in ("wf", "wi", "wc", "wo", "uf", "ui", "uc", "uo")}
    win_mir = nc.dram_tensor("m_win", [24, NCH, 128, 128], BF16)
    win_smir = nc.dram_tensor("m_wins", [D, 32], BF16)
    wout_mir = nc.dram_tensor("m_wout", [REP, NCH, 128, 128], BF16)
    inb = [nc.dram_tensor(f"xin{r}", [1, 128 * TOTC], F32) for r in range(NROUND)]
    outb = [nc.dram_tensor(f"xout{r}", [NCORE, 128 * TOTC], F32) for r in range(NROUND)]

    s_ld = Sem("ld"); s_ws = Sem("ws"); s_gx = Sem("gx"); s_cc = Sem("cc")
    s_v = Sem("v"); s_s = Sem("s"); s_t = Sem("t"); s_st = Sem("st")
    s_gi = Sem("gi"); s_gb = Sem("gb"); s_lc = Sem("lc"); s_e = Sem("e")
    s_pc = Sem("pc")
    s_w = [Sem(f"w{i}") for i in range(4)]
    sems = [s_ld, s_ws, s_gx, s_cc, s_v, s_s, s_t, s_st, s_gi, s_gb, s_lc,
            s_e, s_pc] + s_w

    ops = []

    es = contextlib.ExitStack()
    with es:
        def sbt(name, shape, dt):
            return es.enter_context(nc.sbuf_tensor(name, list(shape), dt))

        x = sbt("x_sb", (128, NCH, T), F32)
        xln = sbt("xln_sb", (128, NCH, T), BF16)
        ones128 = sbt("ones128", (128, 128), BF16)
        ones_bf = sbt("ones_bf", (128, LS), BF16)
        cst = sbt("cst", (128, 4), F32)
        bias3 = sbt("bias3_sb", (128, NCH, 3), F32)
        bov = sbt("bov_sb", (128, NCH), F32)
        in_c = sbt("inc_sb", (128, NCH), F32)
        ck_kv = sbt("ckkv_sb", (128, KS, REP), F32)
        ck_s = sbt("cks_sb", (32, KS), F32)
        theta2p = sbt("theta2p", (128, REP, M), F32)
        score = sbt("score_sb", (32, 1), F32)
        slopes = sbt("slopes_sb", (32, 1), F32)
        ysc_re = sbt("yscre", (128, 1), F32)
        ysc_im = sbt("yscim", (128, 1), F32)
        tmpc1 = sbt("tmpc1", (128, 1), F32)
        tmpc2 = sbt("tmpc2", (128, 1), F32)
        pos32 = sbt("pos32", (32, LT), F32)
        maskb = sbt("maskb", (128, NCORE), F32)
        halob = sbt("halob", (128, HALO), F32)
        slope_aff = sbt("slope_aff", (32, 1), F32)
        c_aff = sbt("c_aff", (32, 1), F32)
        sp_t = sbt("sp_t", (32, 1), F32)
        dmask = sbt("dmask", (32, 1), F32)
        erep8 = sbt("erep8", (32, REP, 128), BF16)
        ei32 = sbt("ei32", (32, 128), I32)
        ej32 = sbt("ej32", (32, 128), I32)
        win_s = sbt("win_s", (128, NCH, 32), BF16)


        af32 = sbt("af32", (128, 9280), F32)
        abf = sbt("abf", (128, 17920), BF16)

        cstate = af32[:, 0:4128].rearrange("p (c w) -> p c w", c=NCH)
        gate = af32[:, 4128:5676].rearrange("p (c w) -> p c w", c=3)
        gtmp = af32[:, 5676:6192]
        cbf = abf[:, 0:4128].rearrange("p (c w) -> p c w", c=NCH)
        wring = [sbt(f"wring{i}", (128, NCH, 128), BF16) for i in range(4)]

        kvb = af32[:, 0:4096].rearrange("p (r l) -> p r l", r=REP)
        tfp = [af32[:, 4096:4608], af32[:, 8256:8768]]
        fap = [af32[:, 5120:5632], af32[:, 8768:9280]]
        tfb = tfp[0]
        fab = fap[0]
        orim = af32[:, 4608:5120].bitcast(BF16)
        scomp = af32[0:32, 5632:6148]
        logtw = af32[0:32, 6192:6704]
        pwc = af32[0:32, 6704:7216]
        denc = af32[0:32, 7216:7728]
        invc = af32[0:32, 7728:8240]
        ei32 = af32[0:32, 4096:4224].bitcast(I32)
        ej32 = af32[0:32, 4224:4352].bitcast(I32)
        lnmu = af32[:, 6704:6962]
        lnu = af32[:, 6962:7220]
        lns = af32[:, 7220:7478]
        lnr = af32[:, 7478:7736]
        lnc = af32[:, 7736:7994]
        fcos = abf[:, 0:4096].rearrange("p (m l) -> p m l", m=M)
        fsin = abf[:, 4096:8192].rearrange("p (m l) -> p m l", m=M)
        s1p = [abf[:, 8192:8704], abf[:, 16896:17408]]
        s2p = [abf[:, 8704:9216], abf[:, 17408:17920]]
        s1b = s1p[0]
        s2b = s2p[0]
        s1qb = abf[:, 9216:9728]
        sinvb = abf[:, 9728:10240]
        cosvb = abf[:, 10240:10752]
        yb = abf[:, 10752:14848].rearrange("p (r l) -> p r l", r=REP)
        pwsl = abf[:, 14848:15360]
        invhi = abf[0:32, 15360:15872]
        invlo = abf[0:32, 15872:16384]
        pwbf = abf[0:32, 16384:16896]
        qb = sbt("qb", (128, 2 * M, LS), BF16)
        tots = [sbt(f"tots{i}", (128, TOTC), F32) for i in range(RING)]
        recvb = [sbt(f"recv{i}", (128, NCORE, TOTC), F32) for i in range(RING)]
        carryb = [sbt(f"carry{i}", (128, TOTC), F32) for i in range(RING)]
        qb_fl = qb[:].rearrange("p a l -> p (a l)")
        xbf = qb_fl[:, 0:NCH * 258].rearrange("p (c n) -> p c n", c=NCH)
        xsq = qb_fl[:, NCH * 258:2 * NCH * 258].rearrange("p (c n) -> p c n", c=NCH)
        cast_st = abf[:, 8192:12288]
        zch = abf[:, 4096:5128].bitcast(F32)
        kip = [fap[0].bitcast(I32), fap[1].bitcast(I32)]
        nreb = sinvb
        nimb = cosvb
        pab = sinvb
        pbb = cosvb
        ycb = abf[:, 4096:5120].bitcast(F32)
        fab2 = abf[:, 5120:6144].bitcast(F32)
        scv = denc

        psb = [es.enter_context(nc.psum_tensor(f"psb{i}", [128, 512], F32))
               for i in range(8)]

        oreb = orim[:, 0:512]
        oimb = orim[:, 512:1024]

        for s in sems:
            s.h = es.enter_context(nc.semaphore(s.name))

        nc.const_aps.register(F32, PI / 2, cst[:, 0:1])
        nc.const_aps.register(F32, -PI, cst[:, 1:2])
        nc.const_aps.register(F32, PI, cst[:, 2:3])

        blk = es.enter_context(nc.Block())

        def addop(engname, sem, k, fns, waits):
            wt = tuple((s.h, v) for (s, v) in waits if v > 0)
            if k is None:
                tgt = sem.n + len(fns)
                sem.n = tgt
                def run(e, fns=tuple(fns), wt=wt, sem=sem):
                    for h, v in wt:
                        e.wait_ge(h, v)
                    for f in fns:
                        f(e).then_inc(sem.h, 1)
            else:
                tgt = sem.n + k
                sem.n = tgt
                def run(e, fn=fns, wt=wt, sem=sem, k=k):
                    for h, v in wt:
                        e.wait_ge(h, v)
                    fn(e).then_inc(sem.h, k)
            ops.append((engname, run))
            return tgt

        vlast = [0]
        def vb(fns, waits=()):
            w = tuple(waits) + ((s_v, vlast[0]),)
            tgt = addop("vector", s_v, None, fns if isinstance(fns, list) else [fns], w)
            vlast[0] = tgt
            return tgt

        def sb_(fns, waits=()):
            return addop("scalar", s_s, None,
                         fns if isinstance(fns, list) else [fns], waits)

        def seb_(fn, waits=()):
            return addop("scalar", s_e, None, [fn], waits)

        def tb(fn, waits=()):
            return addop("tensor", s_t, 1, fn, waits)

        def syncdma(dst, src, waits=(), sem=None):
            sem = sem or s_ld
            tgt = sem.n + 16
            sem.n = tgt
            wt = tuple((s.h, v) for (s, v) in waits if v > 0)
            def run(e, dst=dst, src=src, wt=wt, sem=sem):
                for h, v in wt:
                    e.wait_ge(h, v)
                e.dma_start(out=dst, in_=src).then_inc(sem.h, 16)
            ops.append(("sync", run))
            return tgt

        def gpdma(dst, src, waits=(), sem=None):
            sem = sem or s_ws
            tgt = sem.n + 16
            sem.n = tgt
            wt = tuple((s.h, v) for (s, v) in waits if v > 0)
            def run(e, dst=dst, src=src, wt=wt, sem=sem):
                for h, v in wt:
                    e.wait_ge(h, v)
                e.dma_start(out=dst, in_=src).then_inc(sem.h, 16)
            ops.append(("gpsimd", run))
            return tgt

        def gop(fns, waits=(), sem=None):
            sem = sem or s_gi
            return addop("gpsimd", sem, None,
                         fns if isinstance(fns, list) else [fns], waits)

        def gcc(fn, waits=()):
            return addop("gpsimd", s_cc, 1, fn, waits)

        def syncwait(waits):
            wt = tuple((s.h, v) for (s, v) in waits if v > 0)
            def run(e, wt=wt):
                for h, v in wt:
                    e.wait_ge(h, v)
            ops.append(("sync", run))

        syncdma(x[:], dr["x"])
        for nm, dst in [("bias3", bias3), ("bov", bov), ("in_c", in_c),
                        ("ck_kv", ck_kv), ("ck_s", ck_s), ("score", score),
                        ("slopes", slopes), ("mask", maskb), ("halo", halob),
                        ("theta", theta2p)]:
            syncdma(dst[:], dr[nm])
        syncdma(pos32[:], dr["pos"].broadcast_to([32, LT]))
        syncdma(ysc_re[:], dr["nsc_re"])
        syncdma(ysc_im[:], dr["nsc_im"])
        syncdma(tmpc1[:], dr["wre_d"])
        syncdma(tmpc2[:], dr["wim_d"])
        LD0 = s_ld.n

        V_CONST = vb([lambda e: e.memset(ones128[:], 1.0),
                      lambda e: e.memset(ones_bf[:], 1.0),
                      lambda e: e.memset(cst[:, 0:1], PI / 2),
                      lambda e: e.memset(cst[:, 1:2], -PI),
                      lambda e: e.memset(cst[:, 2:3], PI)]
                     + [(lambda e, i=i: e.memset(tots[i][:], 0.0)) for i in range(RING)])

        G_IOTA = gop([lambda e: e.iota(ei32[:], pattern=[[8, 4], [0, 32]], base=0,
                                       channel_multiplier=0),
                      lambda e: e.iota(ej32[:], pattern=[[0, 128]], base=0,
                                      channel_multiplier=1)])

        V_PRE = vb([lambda e: e.tensor_scalar(theta2p[:], theta2p[:],
                                              float(1.0 / (2 * PI)), None, ALU.mult),
                    lambda e: e.tensor_tensor(ysc_re[:], ysc_re[:], tmpc1[:], ALU.mult),
                    lambda e: e.tensor_tensor(ysc_im[:], ysc_im[:], tmpc2[:], ALU.mult)],
                   waits=[(s_ld, LD0)])
        for r in range(REP):
            vb(lambda e, r=r: e.tensor_scalar(lnu[0:32, 0:128], ei32[:], r, None,
                                              ALU.add), waits=[(s_gi, G_IOTA)])
            vb(lambda e, r=r: e.tensor_tensor(erep8[:, r, :], lnu[0:32, 0:128],
                                              ej32[:], ALU.is_equal))
        G_ROWI = gop(lambda e: e.iota(ej32[:, 0:1], pattern=[[0, 1]], base=0,
                                      channel_multiplier=1), waits=[(s_v, vlast[0])])
        S_AB = sb_(lambda e: e.activation(tmpc1[0:32, :], slopes[:], AF.Abs),
                   waits=[(s_ld, LD0), (s_v, V_PRE)])
        S_EX = sb_(lambda e: e.activation(tmpc1[0:32, :], tmpc1[0:32, :], AF.Exp,
                                          scale=-1.0), waits=[(s_s, S_AB)])
        V_A1 = vb(lambda e: e.tensor_scalar(tmpc2[0:32, :], tmpc1[0:32, :], 1.0,
                                            None, ALU.add), waits=[(s_s, S_EX)])
        S_LN = sb_(lambda e: e.activation(tmpc1[0:32, :], tmpc2[0:32, :], AF.Ln),
                   waits=[(s_v, V_A1)])
        V_A2 = vb(lambda e: e.tensor_scalar(tmpc2[0:32, :], slopes[:], 0.0,
                                            None, ALU.max), waits=[(s_s, S_LN)])
        V_SP = vb(lambda e: e.tensor_tensor(sp_t[:], tmpc1[0:32, :],
                                            tmpc2[0:32, :], ALU.add),
                  waits=[(s_s, S_LN)])
        vb(lambda e: e.tensor_scalar(dmask[:], ej32[0:32, 0:1], 28, None, ALU.is_lt),
           waits=[(s_gi, G_ROWI)])
        vb(lambda e: e.tensor_scalar(lnu[0:32, 0:1], dmask[:], 2.0, -1.0,
                                     ALU.mult, ALU.add), waits=[(s_v, V_SP)])
        vb([lambda e: e.tensor_tensor(slope_aff[:], sp_t[:], lnu[0:32, 0:1], ALU.mult),
            lambda e: e.tensor_tensor(c_aff[:], sp_t[:], dmask[:], ALU.mult)])
        V_AFF = vb(lambda e: e.tensor_scalar(c_aff[:], c_aff[:], -float(L - 1),
                                             None, ALU.mult))

        bankg = [[] for _ in range(8)]
        slotg = [[] for _ in range(4)]

        lnbusy = [[(s_t, 0), (s_s, 0)]]
        def emit_ln(off, wd, tile, xw_extra=()):
            busy = list(lnbusy[0]) + [(s_v, vlast[0])]
            last = 0
            first_write = True
            for it in range(wd // tile):
                o = off + it * tile
                t_c = gpdma(xbf[:, :, 0:tile], x[:, :, o:o + tile],
                            waits=tuple(busy), sem=s_lc)
                S_SQ = sb_([(lambda e, ch=ch, tile=tile: e.activation(
                    xsq[:, ch, 0:tile], xbf[:, ch, 0:tile], AF.Square))
                    for ch in range(NCH)],
                    waits=[(s_lc, t_c)] + busy)
                def statmm(e, tile=tile):
                    for ch in range(NCH):
                        e.matmul(psb[6][:, 0:tile], ones128[:], xbf[:, ch, 0:tile],
                                 start=(ch == 0), stop=(ch == NCH - 1))
                    for ch in range(NCH):
                        i1 = e.matmul(psb[7][:, 0:tile], ones128[:], xsq[:, ch, 0:tile],
                                      start=(ch == 0), stop=(ch == NCH - 1))
                    return i1
                T_STAT = tb(statmm, waits=[(s_lc, t_c), (s_s, S_SQ), (s_v, V_CONST)]
                            + bankg[6] + bankg[7])
                v1 = vb([lambda e, tile=tile: e.tensor_scalar(
                            lnmu[:, 0:tile], psb[6][:, 0:tile], float(1.0 / D),
                            None, ALU.mult),
                         lambda e, tile=tile: e.tensor_scalar(
                            lnu[:, 0:tile], psb[7][:, 0:tile], float(1.0 / D),
                            1e-5, ALU.mult, ALU.add)],
                        waits=[(s_t, T_STAT)])
                bankg[6] = [(s_v, v1)]
                bankg[7] = [(s_v, v1)]
                vb(lambda e, tile=tile: e.tensor_tensor(
                    lnc[:, 0:tile], lnmu[:, 0:tile], lnmu[:, 0:tile], ALU.mult))
                v3 = vb(lambda e, tile=tile: e.tensor_tensor(
                    lnu[:, 0:tile], lnu[:, 0:tile], lnc[:, 0:tile], ALU.subtract))
                s4 = sb_(lambda e, tile=tile: e.activation(
                    lns[:, 0:tile], lnu[:, 0:tile], AF.Ln), waits=[(s_v, v3)])
                s5 = sb_(lambda e, tile=tile: e.activation(
                    lnr[:, 0:tile], lns[:, 0:tile], AF.Exp, scale=-0.5),
                    waits=[(s_s, s4)])
                vb(lambda e: e.memset(cst[:, 3:4], 0.0), waits=[(s_s, s5)])
                for ch in range(NCH):
                    xw = tuple(xw_extra) if first_write else ()
                    first_write = False
                    vb(lambda e, ch=ch, o=o, tile=tile: e.tensor_tensor(
                        lnc[:, 0:tile], x[:, ch, o:o + tile], lnmu[:, 0:tile],
                        ALU.subtract), waits=xw)
                    last = vb(lambda e, ch=ch, o=o, tile=tile: e.tensor_tensor(
                        xln[:, ch, o:o + tile], lnc[:, 0:tile], lnr[:, 0:tile],
                        ALU.mult))
                busy = [(s_t, T_STAT), (s_s, S_SQ)]
            lnbusy[0] = busy
            return last

        slot_i = [0]
        def slotdma(sl, src, waits):
            sem = s_w[sl]
            tgt = sem.n + 16
            sem.n = tgt
            wt = tuple((s.h, v) for (s, v) in waits if v > 0)
            def run(e, src=src, wt=wt, sem=sem, sl=sl):
                for h, v in wt:
                    e.wait_ge(h, v)
                e.dma_start(out=wring[sl][:], in_=src).then_inc(sem.h, 16)
            ops.append(("sync", run))
            return tgt

        def stream_w(nm, och):
            sl = slot_i[0] % 4
            slot_i[0] += 1
            tg = slotdma(sl, wmir[nm].ap()[och].rearrange("c p n -> p c n"),
                         waits=[PRE[nm]] + slotg[sl])
            return sl, tg

        def stream_piece(idx):
            sl = slot_i[0] % 4
            slot_i[0] += 1
            tg = slotdma(sl, win_mir.ap()[idx].rearrange("c p n -> p c n"),
                         waits=[PRE["w_in"]] + slotg[sl])
            return sl, tg

        def emit_ffn(regions, tile, ln_tgt, reg_done=None):
            Wd0 = regions[0][1]
            S_C0 = sb_([(lambda e, ch=ch, Wd0=Wd0: e.activation(
                cstate[:, ch, 0:Wd0], x[:, 0, 0:Wd0], AF.Identity,
                bias=in_c[:, ch:ch + 1], scale=0.0)) for ch in range(NCH)],
                waits=[(s_ld, LD0), (s_v, vlast[0]), (s_t, s_t.n)])
            V_CB = vb([(lambda e, ch=ch, Wd0=Wd0: e.tensor_copy(
                cbf[:, ch, 0:Wd0], cstate[:, ch, 0:Wd0])) for ch in range(NCH)],
                waits=[(s_s, S_C0)])
            gnames = ("wf", "wi", "wc")
            unames = ("uf", "ui", "uc")
            prev_cb = V_CB
            for t_i, (xo, Wd) in enumerate(regions):
                nt = Wd // tile
                for och in range(NCH):
                    act_waits = []
                    for g in range(3):
                        pw_ = stream_w(gnames[g], och)
                        pu_ = stream_w(unames[g], och)
                        banks = [g * 2 + it for it in range(nt)]
                        bgw = []
                        for bk in banks:
                            bgw += bankg[bk]
                        def gmm(e, xo=xo, Wd=Wd, tile=tile,
                                pw_=pw_, pu_=pu_, banks=tuple(banks), nt=nt):
                            for it in range(nt):
                                no = it * tile
                                for ich in range(NCH):
                                    e.matmul(
                                        psb[banks[it]][:, 0:tile],
                                        wring[pw_[0]][:, ich, :],
                                        xln[:, ich, xo + no:xo + no + tile],
                                        start=(ich == 0), stop=False)
                                for ich in range(NCH):
                                    i1 = e.matmul(
                                        psb[banks[it]][:, 0:tile],
                                        wring[pu_[0]][:, ich, :],
                                        cbf[:, ich, no:no + tile],
                                        start=False, stop=(ich == NCH - 1))
                            return i1
                        T_G = tb(gmm, waits=[(s_w[pw_[0]], pw_[1]),
                                             (s_w[pu_[0]], pu_[1]),
                                             (s_v, prev_cb), (s_v, ln_tgt)]
                                 + bgw)
                        slotg[pw_[0]] = [(s_t, T_G)]
                        slotg[pu_[0]] = [(s_t, T_G)]
                        S_A = sb_([(lambda e, g=g, och=och, bank=banks[it2],
                                    no=it2 * tile, tile=tile: e.activation(
                                        gate[:, g, no:no + tile],
                                        psb[bank][:, 0:tile],
                                        AF.Tanh if g == 2 else AF.Sigmoid,
                                        bias=bias3[:, och, g:g + 1]))
                                   for it2 in range(nt)],
                                  waits=[(s_t, T_G), (s_v, vlast[0])])
                        for bk in banks:
                            bankg[bk] = [(s_s, S_A)]
                        act_waits.append((s_s, S_A))
                    vb(lambda e, Wd=Wd: e.tensor_tensor(
                        gtmp[:, 0:Wd], gate[:, 1, 0:Wd], gate[:, 2, 0:Wd],
                        ALU.mult), waits=act_waits)
                    vb(lambda e, och=och, Wd=Wd: e.tensor_tensor(
                        gate[:, 0, 0:Wd], gate[:, 0, 0:Wd],
                        cstate[:, och, 0:Wd], ALU.mult))
                    vb(lambda e, och=och, Wd=Wd: e.tensor_tensor(
                        cstate[:, och, 0:Wd], gtmp[:, 0:Wd], gate[:, 0, 0:Wd],
                        ALU.add))
                prev_cb = vb([(lambda e, ch=ch, Wd=Wd: e.tensor_copy(
                    cbf[:, ch, 0:Wd], cstate[:, ch, 0:Wd])) for ch in range(NCH)],
                    waits=[(s_t, s_t.n)])
            for t_i, (xo, Wd) in enumerate(regions):
                nt = Wd // tile
                for och in range(NCH):
                    slw, tgw = stream_w("wo", och)
                    slu, tgu = stream_w("uo", och)
                    T_LAST = 0
                    for it in range(nt):
                        no = it * tile
                        bank = 6 + (it % 2)
                        def hmm(e, slw=slw, slu=slu, no=no, xo=xo,
                                tile=tile, bank=bank):
                            for ich in range(NCH):
                                e.matmul(
                                    psb[bank][:, 0:tile],
                                    wring[slw][:, ich, :],
                                    xln[:, ich, xo + no:xo + no + tile],
                                    start=(ich == 0), stop=False)
                            for ich in range(NCH):
                                i1 = e.matmul(
                                    psb[bank][:, 0:tile],
                                    wring[slu][:, ich, :],
                                    cbf[:, ich, no:no + tile],
                                    start=False, stop=(ich == NCH - 1))
                            return i1
                        T_H = tb(hmm, waits=[(s_w[slw], tgw), (s_w[slu], tgu),
                                             (s_v, prev_cb)] + bankg[bank])
                        T_LAST = T_H
                        S_H = sb_(lambda e, och=och, bank=bank, no=no, tile=tile:
                                  e.activation(gtmp[:, no:no + tile],
                                               psb[bank][:, 0:tile], AF.Tanh,
                                               bias=bov[:, och:och + 1]),
                                  waits=[(s_t, T_H), (s_v, vlast[0])])
                        bankg[bank] = [(s_s, S_H)]
                        vb(lambda e, och=och, no=no, xo=xo, tile=tile:
                           e.scalar_tensor_tensor(
                               x[:, och, xo + no:xo + no + tile],
                               gtmp[:, no:no + tile], 0.5,
                               x[:, och, xo + no:xo + no + tile],
                               ALU.mult, ALU.add),
                           waits=[(s_s, S_H)])
                    slotg[slw] = [(s_t, T_LAST)]
                    slotg[slu] = [(s_t, T_LAST)]
                if reg_done is not None:
                    reg_done.append(vlast[0])
            return vlast[0]

        lnA = emit_ln(0, T, 258, xw_extra=[(s_v, V_AFF)])
        PRE = {}
        def precast4(dst_ap, src_ap, n0, sem, bar=0):
            for o0 in range(0, n0, 2):
                gpdma(dst_ap[o0:o0 + 2], src_ap[o0:o0 + 2],
                      waits=[(sem, bar)], sem=sem)
        for nm in ("wf", "uf", "wi", "ui", "wc", "uc"):
            precast4(wmir[nm].ap(), dr[nm], NCH, s_pc)
        for nm in ("wf", "uf", "wi", "ui", "wc", "uc"):
            PRE[nm] = (s_pc, s_pc.n)
        bar1 = s_pc.n
        precast4(wmir["wo"].ap(), dr["wo"], NCH, s_pc, bar1)
        precast4(wmir["uo"].ap(), dr["uo"], NCH, s_pc, bar1)
        PRE["wo"] = PRE["uo"] = (s_pc, s_pc.n)
        bar2 = s_pc.n
        precast4(wout_mir.ap(), dr["w_out"], REP, s_pc, bar2)
        PRE["w_out"] = (s_pc, s_pc.n)
        precast4(win_mir.ap(), dr["w_in"], 24, s_ws)
        gpdma(win_smir.ap(), dr["w_in_s"], waits=[], sem=s_ws)
        PRE["w_in"] = (s_ws, s_ws.n)
        t_wins = gpdma(win_s[:], win_smir.ap()
                       .rearrange("(c p) n -> p c n", p=128),
                       waits=[(s_ws, s_ws.n)])
        PRE_ALL = s_ws.n

        ffnA = emit_ffn([(ti * LT, LT) for ti in range(B)], 258, lnA)

        rnd = [0]
        feat_prev = [0]
        XCH_BOUNCE = []
        MASKS = []
        lnb_list = []
        pend_wout = [None]
        pend_se = [None]
        sy_prev = [0]
        for b in range(B):
            lnb_list.append(emit_ln(b * LT, LT, 258, xw_extra=[(s_t, s_t.n)]))
        for b in range(B):
            xo = b * LT
            lnb = lnb_list[b]

            if pend_se[0] is not None:
                pend_se[0]()
                pend_se[0] = None

            for r in range(REP):
                sl, tg = stream_piece(r)
                evac = []
                T_KV2 = 0
                for it in range(2):
                    no = it * 258
                    bank = 5 + it
                    def kvmm(e, sl=sl, no=no, xo=xo, bank=bank):
                        for ich in range(NCH):
                            i1 = e.matmul(psb[bank][:, 0:258],
                                          wring[sl][:, ich, :],
                                          xln[:, ich, xo + no:xo + no + 258],
                                          start=(ich == 0), stop=(ich == NCH - 1))
                        return i1
                    T_KV = tb(kvmm, waits=[(s_w[sl], tg), (s_v, lnb)] + bankg[bank])
                    T_KV2 = T_KV
                    S_E = sb_(lambda e, no=no, bank=bank: e.activation(
                        zch[:, no:no + 258], psb[bank][:, 0:258], AF.Identity),
                        waits=[(s_t, T_KV), (s_v, vlast[0]),
                               (s_e, sy_prev[0])])
                    bankg[bank] = [(s_s, S_E)]
                    evac.append((s_s, S_E))
                slotg[sl] = [(s_t, T_KV2)]
                vb(lambda e: e.tensor_tensor(zch[:, 0:HALO], zch[:, 0:HALO],
                                             halob[:], ALU.mult), waits=evac)
                vb(lambda e, r=r: e.tensor_scalar(kvb[:, r, :], zch[:, 1:1 + LS],
                                                  ck_kv[:, 0, r:r + 1], None,
                                                  ALU.mult))
                for j in range(1, KS):
                    vb(lambda e, r=r, j=j: e.scalar_tensor_tensor(
                        kvb[:, r, :], zch[:, 1 + j:1 + j + LS],
                        ck_kv[:, j, r:r + 1], kvb[:, r, :], ALU.mult, ALU.add))
            KVD = vlast[0]

            if pend_wout[0] is not None:
                pend_wout[0]()
                pend_wout[0] = None

            evac = []
            for it in range(2):
                no = it * 258
                bank = 5 + it
                def smm(e, no=no, xo=xo, bank=bank):
                    for ich in range(NCH):
                        i1 = e.matmul(psb[bank][0:32, 0:258], win_s[:, ich, :],
                                      xln[:, ich, xo + no:xo + no + 258],
                                      start=(ich == 0), stop=(ich == NCH - 1))
                    return i1
                T_S = tb(smm, waits=[(s_ws, t_wins), (s_v, lnb)] + bankg[bank])
                S_E = sb_(lambda e, no=no, bank=bank: e.activation(
                    scomp[:, no:no + 258], psb[bank][0:32, 0:258], AF.Identity),
                    waits=[(s_t, T_S), (s_v, vlast[0])])
                bankg[bank] = [(s_s, S_E)]
                evac.append((s_s, S_E))
            vb(lambda e: e.tensor_tensor(scomp[:, 0:HALO], scomp[:, 0:HALO],
                                         halob[0:32, :], ALU.mult), waits=evac)
            vb(lambda e: e.tensor_scalar(scv, scomp[:, 1:1 + LS], ck_s[:, 0:1],
                                         None, ALU.mult))
            for j in range(1, KS):
                vb(lambda e, j=j: e.scalar_tensor_tensor(
                    scv, scomp[:, 1 + j:1 + j + LS], ck_s[:, j:j + 1], scv,
                    ALU.mult, ALU.add))
            vb(lambda e: e.tensor_scalar(logtw, pos32[:, HALO:HALO + LS],
                                         slope_aff[:], c_aff[:], ALU.mult, ALU.add),
               waits=[(s_ld, LD0)])
            vb(lambda e: e.tensor_scalar(pwc, scv, score[:], 20.0,
                                         ALU.mult, ALU.min))
            V_P3 = vb(lambda e: e.scalar_tensor_tensor(pwc, pwc, -20.0,
                                                       logtw, ALU.max, ALU.add))
            tot_guard = ([(s_gx, XCH_BOUNCE[b * REP - RING])]
                         if b * REP >= RING else [])
            S_PW = sb_(lambda e, b=b: e.activation(
                pwc, pwc, AF.Exp,
                accum_out=tots[(b * REP) % RING][0:32, 16:17]),
                waits=[(s_v, V_P3)] + tot_guard)
            S_PWB = sb_(lambda e: e.activation(pwbf, pwc, AF.Identity),
                        waits=[(s_s, S_PW), (s_t, s_t.n)])

            Q_TGTS = []
            def emit_qproj():
                for j in range(2 * M):
                    sl, tg = stream_piece(8 + j)
                    bank = 7
                    def qmm(e, sl=sl, xo=xo, bank=bank):
                        for ich in range(NCH):
                            i1 = e.matmul(psb[bank][:, 0:512],
                                          wring[sl][:, ich, :],
                                          xln[:, ich, xo + HALO:xo + HALO + LS],
                                          start=(ich == 0), stop=(ich == NCH - 1))
                        return i1
                    T_Q = tb(qmm, waits=[(s_w[sl], tg), (s_v, lnb)] + bankg[bank])
                    slotg[sl] = [(s_t, T_Q)]
                    S_Q = sb_(lambda e, j=j, bank=bank: e.activation(
                        qb[:, j, :], psb[bank][:, 0:512], AF.Identity),
                        waits=[(s_t, T_Q), (s_v, vlast[0])])
                    bankg[bank] = [(s_s, S_Q)]
                    Q_TGTS.append((s_s, S_Q))

            def slice_phaseA(rep):
                r_id = rnd[0]; rnd[0] += 1
                rr = r_id % RING
                tot = tots[rr]; rcv = recvb[rr]; car = carryb[rr]
                tot_guard = ([(s_gx, XCH_BOUNCE[r_id - RING])]
                             if r_id >= RING else [])

                def pwrep_mm(e, rep=rep):
                    return e.matmul(psb[5][:, 0:512], erep8[:, rep, :], pwbf,
                                    start=True, stop=True)
                T_PW = tb(pwrep_mm, waits=[(s_s, S_PWB), (s_v, V_PRE)] + bankg[5])
                S_PWS = sb_(lambda e: e.activation(pwsl, psb[5][:, 0:512],
                                                   AF.Identity),
                            waits=[(s_t, T_PW), (s_v, feat_prev[0])])
                bankg[5] = [(s_s, S_PWS)]

                VF_t = {}; K_t = {}; S1_t = {}; S2_t = {}; FA_t = {}; SINV_t = {}
                for mm in range(M + 2):
                    if mm < M:
                        m = mm; p = m % 2
                        w = []
                        if m == 0:
                            w = [(s_v, KVD), (s_s, S_PWS), (s_ws, PRE_ALL)]
                        if m >= 2:
                            w = w + [(s_v, VF_t[m - 2]), (s_s, S1_t[m - 2])]
                        T1 = sb_(lambda e, rep=rep, m=m, p=p: e.activation(
                            tfp[p], kvb[:, rep, :], AF.Identity,
                            scale=theta2p[:, rep, m:m + 1]), waits=w)
                        w2 = [(s_s, T1)]
                        if m >= 2:
                            w2 = w2 + [(s_s, S2_t[m - 2])]
                        K_t[m] = sb_(lambda e, p=p: e.activation(
                            kip[p], tfp[p], AF.Identity), waits=w2)
                    if mm == 1 and pend_se[0] is not None:
                        pend_se[0]()
                        pend_se[0] = None
                    if 0 <= mm - 1 < M:
                        m = mm - 1; p = m % 2
                        VF_t[m] = vb(lambda e, p=p: e.tensor_tensor(
                            tfp[p], tfp[p], kip[p], ALU.subtract),
                            waits=[(s_s, K_t[m])])
                        FA_t[m] = sb_(lambda e, p=p: e.activation(
                            fap[p], tfp[p], AF.Abs), waits=[(s_v, VF_t[m])])
                        w3 = []
                        if m >= 2:
                            w3 = [(s_v, SINV_t[m - 2])]
                        S1_t[m] = sb_(lambda e, p=p: e.activation(
                            s1p[p], tfp[p], AF.Sin, scale=PI), waits=w3)
                        S2_t[m] = sb_(lambda e, p=p: e.activation(
                            s2p[p], fap[p], AF.Sin, bias=PI / 2, scale=-PI),
                            waits=[(s_s, FA_t[m])])
                    if 0 <= mm - 2 < M:
                        m = mm - 2; p = m % 2
                        w4 = [(s_s, S1_t[m])]
                        if m == 0:
                            w4 = w4 + tot_guard
                        vb(lambda e, p=p: e.tensor_tensor(
                            s1qb, s1p[p], s1p[p], ALU.mult), waits=w4)
                        SINV_t[m] = vb(lambda e, p=p: e.tensor_tensor(
                            sinvb, s1p[p], s2p[p], ALU.mult),
                            waits=[(s_s, S2_t[m])])
                        vb(lambda e: e.tensor_scalar(cosvb, s1qb, -2.0, 1.0,
                                                     ALU.mult, ALU.add))
                        wfs = [(s_e, sy_prev[0])] if m < 2 else []
                        vb(lambda e, m=m, tot=tot: e.scalar_tensor_tensor(
                            fsin[:, m, :], sinvb, 2.0, pwsl, ALU.mult, ALU.mult,
                            accum_out=tot[:, 8 + m:9 + m]), waits=wfs)
                        vb(lambda e, m=m, tot=tot: e.scalar_tensor_tensor(
                            fcos[:, m, :], cosvb, 1.0, pwsl, ALU.mult, ALU.mult,
                            accum_out=tot[:, m:m + 1]))
                FEAT = vlast[0]
                feat_prev[0] = FEAT

                bounce = syncdma(inb[r_id].ap(), tot[:],
                                 waits=[(s_v, FEAT), (s_s, S_PW)], sem=s_gx)
                XCH_BOUNCE.append(bounce)
                CCT = gcc(lambda e, r_id=r_id: e.collective_compute(
                    "AllGather", ALU.bypass,
                    replica_groups=[list(range(NCORE))],
                    ins=[inb[r_id].ap().opt()],
                    outs=[outb[r_id].ap().opt()]),
                    waits=[(s_gx, bounce)])
                rb_guard = [MASKS[r_id - RING]] if r_id >= RING else []
                readback = syncdma(rcv[:],
                                   outb[r_id].ap().rearrange("s (p c) -> p s c", p=128),
                                   waits=[(s_cc, CCT)] + rb_guard, sem=s_gb)

                for m in range(M):
                    vb([lambda e, m=m: e.tensor_tensor_scan(
                            fcos[:, m, :], ones_bf[:], fcos[:, m, :], 0.0,
                            ALU.mult, ALU.add),
                        lambda e, m=m: e.tensor_tensor_scan(
                            fsin[:, m, :], ones_bf[:], fsin[:, m, :], 0.0,
                            ALU.mult, ALU.add)])
                if rep == 0:
                    vb(lambda e: e.tensor_tensor_scan(
                        denc, ones_bf[0:32, :], pwc, 0.0, ALU.mult, ALU.add))
                return dict(rep=rep, r_id=r_id, rcv=rcv, car=car,
                            readback=readback)

            def slice_phaseB(ctx):
                rep = ctx["rep"]; r_id = ctx["r_id"]
                rcv = ctx["rcv"]; car = ctx["car"]
                readback = ctx["readback"]

                vb(lambda e, car=car, rcv=rcv: e.tensor_scalar(
                    car[:], rcv[:, 0, :], maskb[:, 0:1], None, ALU.mult),
                   waits=[(s_gb, readback)])
                for jc in range(1, NCORE):
                    vb(lambda e, jc=jc, car=car, rcv=rcv: e.scalar_tensor_tensor(
                        car[:], rcv[:, jc, :], maskb[:, jc:jc + 1], car[:],
                        ALU.mult, ALU.add))
                VMASK = vlast[0]
                MASKS.append((s_v, VMASK))

                CAD = []
                for m in range(M):
                    wca = [(s_v, VMASK)] if m == 0 else []
                    sb_(lambda e, m=m, car=car: e.activation(
                        fcos[:, m, :], fcos[:, m, :], AF.Identity,
                        bias=car[:, m:m + 1]), waits=wca)
                    CAD.append(sb_(lambda e, m=m, car=car: e.activation(
                        fsin[:, m, :], fsin[:, m, :], AF.Identity,
                        bias=car[:, 8 + m:9 + m])))
                if rep == 0:
                    vb(lambda e, car=car: e.tensor_scalar(
                        denc, denc, car[0:32, 16:17], None, ALU.add))
                    vb(lambda e: e.tensor_scalar(invc, denc, 1e-4, None,
                                                 ALU.max))
                    vb(lambda e: e.reciprocal(denc, invc))
                    vb(lambda e: e.tensor_copy(invhi, denc))
                    vb(lambda e: e.tensor_tensor(invlo, denc, invhi,
                                                 ALU.subtract))
                INVD = vlast[0]

                PROD3 = 0
                for m in range(M):
                    vb([lambda e, m=m: e.tensor_tensor(pab, fcos[:, m, :],
                                                       qb[:, 2 * m, :], ALU.mult),
                        lambda e, m=m: e.tensor_tensor(pbb, fsin[:, m, :],
                                                       qb[:, 2 * m + 1, :], ALU.mult)],
                       waits=[Q_TGTS[2 * m + 1], (s_s, CAD[m])])
                    if m == 0:
                        vb(lambda e: e.tensor_scalar(oreb, pab, 1.0, None,
                                                     ALU.mult))
                    else:
                        vb(lambda e: e.tensor_tensor(oreb, oreb, pab,
                                                     ALU.add))
                    vb(lambda e: e.tensor_tensor(oreb, oreb, pbb, ALU.add))
                    vb([lambda e, m=m: e.tensor_tensor(pab, fsin[:, m, :],
                                                       qb[:, 2 * m, :], ALU.mult),
                        lambda e, m=m: e.tensor_tensor(pbb, fcos[:, m, :],
                                                       qb[:, 2 * m + 1, :], ALU.mult)])
                    if m == 3:
                        PROD3 = vlast[0]
                    if m == 0:
                        vb(lambda e: e.tensor_scalar(oimb, pab, 1.0, None,
                                                     ALU.mult))
                    else:
                        vb(lambda e: e.tensor_tensor(oimb, oimb, pab,
                                                     ALU.add))
                    vb(lambda e: e.tensor_tensor(oimb, oimb, pbb,
                                                 ALU.subtract))
                OD = vlast[0]

                def invrep_mm(e, rep=rep):
                    e.matmul(psb[6][:, 0:512], erep8[:, rep, :], invhi,
                             start=True, stop=False)
                    return e.matmul(psb[6][:, 0:512], erep8[:, rep, :], invlo,
                                    start=False, stop=True)
                T_INV = tb(invrep_mm, waits=[(s_v, INVD)] + bankg[6])
                ir_tgt = 2 * r_id + 1
                sy_tgt = 2 * r_id + 2
                bankg[6] = [(s_e, ir_tgt)]
                yab = kvb[:, rep, :]
                vb([lambda e, yab=yab: e.tensor_tensor(yab, oreb, fab2,
                                                       ALU.mult),
                    lambda e: e.tensor_tensor(ycb, oimb, fab2, ALU.mult)],
                   waits=[(s_e, ir_tgt)])
                vb(lambda e, yab=yab: e.tensor_scalar(yab, yab, ysc_re[:],
                                                      None, ALU.mult))
                V_YE = vb(lambda e, yab=yab: e.scalar_tensor_tensor(
                    ycb, ycb, ysc_im[:], yab, ALU.mult, ALU.add))
                vb(lambda e: e.memset(lnc[:, 0:1], 0.0))
                sy_prev[0] = sy_tgt

                def emit_epi(rep=rep, T_INV=T_INV, PROD3=PROD3, V_YE=V_YE,
                             ir_tgt=ir_tgt, sy_tgt=sy_tgt):
                    t1 = seb_(lambda e: e.activation(fab2, psb[6][:, 0:512],
                                                     AF.Identity),
                              waits=[(s_t, T_INV), (s_v, PROD3)])
                    t2 = seb_(lambda e, rep=rep: e.activation(
                        yb[:, rep, :], ycb, AF.Silu), waits=[(s_v, V_YE)])
                    assert t1 == ir_tgt and t2 == sy_tgt
                pend_se[0] = emit_epi

            ctx0 = slice_phaseA(0)
            emit_qproj()
            slice_phaseB(ctx0)
            for rep in range(1, REP):
                slice_phaseB(slice_phaseA(rep))

            def make_wout(xo=xo, YD=vlast[0], sy_last=2 * rnd[0]):
                def emit_wout():
                    T_WOUT = 0
                    for r in range(REP):
                        sl = slot_i[0] % 4
                        slot_i[0] += 1
                        tg = slotdma(sl, wout_mir.ap()[r].rearrange("c p n -> p c n"),
                                     waits=[PRE["w_out"]] + slotg[sl])
                        bg = []
                        if r == 0:
                            for dch in range(NCH):
                                bg += bankg[dch]
                        def womm(e, sl=sl, r=r):
                            for dch in range(NCH):
                                i1 = e.matmul(psb[dch][:, 0:512], wring[sl][:, dch, :],
                                              yb[:, r, :], start=(r == 0), stop=(r == REP - 1))
                            return i1
                        T_WOUT = tb(womm, waits=[(s_w[sl], tg), (s_v, YD),
                                                 (s_e, sy_last)] + bg)
                        slotg[sl] = [(s_t, T_WOUT)]
                    for dch in range(NCH):
                        vt = vb(lambda e, dch=dch, xo=xo: e.scalar_tensor_tensor(
                            x[:, dch, xo + HALO:xo + HALO + LS], psb[dch][:, 0:512], 1.0,
                            x[:, dch, xo + HALO:xo + HALO + LS], ALU.mult, ALU.add),
                            waits=[(s_t, T_WOUT)])
                        bankg[dch] = [(s_v, vt)]
                return emit_wout
            pend_wout[0] = make_wout()
        pend_se[0]()
        pend_se[0] = None
        pend_wout[0]()
        pend_wout[0] = None

        lnB = 0
        for b in range(B):
            lnB = emit_ln(b * LT + HALO, LS, 256, xw_extra=[(s_t, s_t.n)])
        regB = []
        ffnB = emit_ffn([(ti * LT + HALO, LS) for ti in range(B)], 256, lnB,
                        reg_done=regB)

        st_t = 0
        for b in range(B):
            for ch in range(NCH):
                tgt = s_st.n + 16
                s_st.n = tgt
                def run(e, ch=ch, b=b, w=regB[b]):
                    e.wait_ge(s_v.h, w)
                    e.dma_start(out=out_h[:, ch, b, :],
                                in_=x[:, ch, b * LT + HALO:b * LT + HALO + LS]
                                ).then_inc(s_st.h, 16)
                ops.append(("sync", run))
                st_t = tgt
        syncwait([(s_st, st_t)])

        @blk.sync
        def _(e):
            for en, fn in ops:
                if en == "sync":
                    fn(e)

        @blk.gpsimd
        def _(e):
            for en, fn in ops:
                if en == "gpsimd":
                    fn(e)

        @blk.vector
        def _(e):
            for en, fn in ops:
                if en == "vector":
                    fn(e)

        @blk.scalar
        def _(e):
            for en, fn in ops:
                if en == "scalar":
                    fn(e)

        @blk.tensor
        def _(e):
            for en, fn in ops:
                if en == "tensor":
                    fn(e)

    return nc


def _stage_core(core, xf, win, win_s, wout, ffn_t, bias_t, conv_k, theta,
                slopes_cat, score_s, nsc, wre, wim):
    lo = LS * core - HALO
    xs = np.zeros((128, NCH, T), np.float32)
    for b in range(B):
        lo2 = max(lo, 0)
        seg = xf[b, lo2:LS * (core + 1), :]
        segT = np.ascontiguousarray(seg.T)
        n = seg.shape[0]
        for ch in range(NCH):
            xs[:, ch, b * LT + (LT - n):(b + 1) * LT] = segT[ch * 128:(ch + 1) * 128]
    pos = np.arange(lo, LS * (core + 1), dtype=np.float32).reshape(1, LT)
    mask = np.zeros((128, NCORE), np.float32)
    mask[:, :core] = 1.0
    halo = np.full((128, HALO), 0.0 if core == 0 else 1.0, np.float32)
    d = {"x": xs, "w_in": win, "w_in_s": win_s, "w_out": wout, "pos": pos,
         "mask": mask,
         "halo": halo, "theta": theta, "score": score_s, "slopes": slopes_cat,
         "ck_kv": conv_k[0], "ck_s": conv_k[1],
         "nsc_re": nsc[0], "nsc_im": nsc[1], "wre_d": wre, "wim_d": wim}
    d.update(ffn_t)
    d.update(bias_t)
    return d


def kernel(**inputs):
    g = {k: np.asarray(v, np.float32) for k, v in inputs.items()}
    xf = g["L0_attn_x"]
    W_in = g["L0_attn_W_in"]
    conv_k = g["L0_attn_conv_k"]
    theta_in = g["L0_attn_theta"].reshape(K, 32, M)
    nsc = g["L0_attn_norm_scale"]
    W_out = g["L0_attn_W_out"]

    pidx = np.arange(128)
    kqv, hv = pidx // 32, pidx % 32
    kv_cols = np.empty((REP, 128), np.int64)
    for r in range(REP):
        kv_cols[r] = (kqv * 8 + r) * 32 + hv
    q_cols = np.empty((2 * M, 128), np.int64)
    for m in range(M):
        for c in range(2):
            q_cols[2 * m + c] = 1056 + (pidx * 8 + m) * 2 + c
    perm = np.concatenate([kv_cols.reshape(-1), 1024 + np.arange(32),
                           q_cols.reshape(-1)])
    win = W_in[:, perm]
    win_p = np.empty((24, NCH, 128, 128), np.float32)
    for i, c0 in enumerate(list(range(0, 1024, 128)) +
                           list(range(1056, 3104, 128))):
        win_p[i] = win[:, c0:c0 + 128].reshape(NCH, 128, 128)
    win_sh = np.ascontiguousarray(win[:, 1024:1056])

    wout = np.empty((REP, 128, D), np.float32)
    for r in range(REP):
        wout[r] = W_out[(kqv * 8 + r) * 32 + hv, :]
    wout = np.ascontiguousarray(
        wout.reshape(REP, 128, NCH, 128).transpose(0, 2, 1, 3))

    ckkv = np.empty((128, KS, REP), np.float32)
    for j in range(KS):
        for r in range(REP):
            ckkv[:, j, r] = conv_k[j, 0, (kqv * 8 + r) * 32 + hv]
    cks = np.ascontiguousarray(conv_k[:, 0, 1024:1056].T)

    theta = np.empty((128, REP, M), np.float32)
    for r in range(REP):
        theta[:, r, :] = theta_in[kqv * 8 + r, hv, :]

    slopes_cat = np.concatenate([g["L0_attn_decay_slopes"],
                                 g["L0_attn_anchor_slopes"]]).reshape(32, 1)
    score_s = g["L0_attn_score_scale"].reshape(32, 1)
    nsc_re = np.ascontiguousarray(np.tile(nsc[:32], 4).reshape(128, 1))
    nsc_im = np.ascontiguousarray(np.tile(nsc[32:], 4).reshape(128, 1))
    wre_d = np.ascontiguousarray(np.tile(np.diagonal(g["L0_attn_W_re"]), 4)
                                 .reshape(128, 1))
    wim_d = np.ascontiguousarray(np.tile(np.diagonal(g["L0_attn_W_im"]), 4)
                                 .reshape(128, 1))

    ffn_t = {}
    for nm in ("wf", "wi", "wc", "wo", "uf", "ui", "uc", "uo"):
        a = g["L0_ffn_" + nm].T
        ffn_t[nm] = np.ascontiguousarray(
            a.reshape(NCH, 128, NCH, 128).transpose(2, 0, 1, 3))

    def col_layout(v):
        return np.ascontiguousarray(v.reshape(NCH, 128).T)
    bias_t = {
        "bias3": np.ascontiguousarray(np.stack(
            [col_layout(g["L0_ffn_bf"]), col_layout(g["L0_ffn_bi"]),
             col_layout(g["L0_ffn_bc"])], axis=2)),
        "bov": col_layout(g["L0_ffn_bo"]),
        "in_c": col_layout(g["L0_ffn_in_c"]),
    }

    in_maps = [_stage_core(c, xf, win_p, win_sh, wout, ffn_t, bias_t,
                           (ckkv, cks), theta, slopes_cat, score_s,
                           (nsc_re, nsc_im), wre_d, wim_d)
               for c in range(NCORE)]

    nc = build_program()
    res = run_bass_kernel_spmd(nc, in_maps, core_ids=list(range(NCORE)))

    out = np.empty((B, L, D), np.float32)
    for c in range(NCORE):
        o = res.results[c]["out"]
        o = np.transpose(o, (2, 3, 1, 0)).reshape(B, LS, D)
        out[:, LS * c:LS * (c + 1), :] = o
    return out
